# version 14
# baseline (speedup 1.0000x reference)
import numpy as np
import concourse.bass as bass
import concourse.mybir as mybir
from concourse.ap import AP
from concourse.bass_utils import run_bass_kernel_spmd

F32 = mybir.dt.float32
BF16 = mybir.dt.bfloat16
AF = mybir.ActivationFunctionType
ALU = mybir.AluOpType
AX = mybir.AxisListType.X

T, C, F, L = 4096, 1024, 4096, 4
NCB = C // 128
NFB = F // 128
SA, SB = 512, 256
EPS = 1e-5

ENGINES = ("tensor", "vector", "scalar", "gpsimd", "sync")
USE_POOL = True
REPHASE = True
FOLD_PROLOGUE = True


class Sched:

    SELF_SYNC = ("vector", "scalar", "gpsimd")

    def __init__(self, nc):
        self.nc = nc
        self.streams = {e: [] for e in ENGINES}
        self.semkeys = set(ENGINES) | {"gpsimd_dma", "vector_dma",
                                       "scalar_dma"}
        self.sems = {e: nc.alloc_semaphore(f"sem_{e}") for e in self.semkeys}
        self.state = {}
        self.hw = {}
        self.relaxed = set()

    def _sem(self, key):
        if key not in self.sems:
            self.semkeys.add(key)
            self.sems[key] = self.nc.alloc_semaphore(f"sem_{key}")
        return self.sems[key]

    @staticmethod
    def _key(b):
        return b if isinstance(b, (str, tuple, int)) else id(b)

    def _st(self, key):
        return self.state.setdefault(self._key(key), {"w": None, "r": []})

    def _semkey(self, eng, idx):
        ent = self.streams[eng][idx]
        if ent.get("relax"):
            return ent["relax"]
        if ent["dma"] and eng in ("gpsimd", "vector", "scalar"):
            return f"{eng}_dma"
        return eng

    def op(self, eng, fn, reads=(), writes=(), dma=False, relax=None,
           tag=""):
        if relax is not None:
            self._sem(relax)
            self.relaxed.add(relax)
        idx = len(self.streams[eng])
        selfsync = eng in self.SELF_SYNC
        waits = {}

        def need(we, wi):
            if we == eng and not selfsync:
                return
            sk = self._semkey(we, wi)
            k = (we, sk)
            waits[k] = max(waits.get(k, -1), wi)

        for b in reads:
            st = self._st(b)
            if st["w"] is not None:
                need(*st["w"])
        for b in writes:
            st = self._st(b)
            if st["w"] is not None:
                need(*st["w"])
            for re_, ri in st["r"]:
                need(re_, ri)
        final_waits = []
        for (we, sk), wi in waits.items():
            hk = (eng, sk)
            if self.hw.get(hk, -1) >= wi:
                continue
            self.hw[hk] = wi
            self.streams[we][wi]["inc"] = True
            final_waits.append((we, wi))
        self.streams[eng].append(
            {"fn": fn, "waits": final_waits, "inc": dma, "dma": dma,
             "relax": relax, "tag": tag})
        for b in writes:
            st = self._st(b)
            st["w"] = (eng, idx)
            st["r"] = []
        for b in reads:
            self._st(b)["r"].append((eng, idx))
        return idx

    def alias(self, old_keys, new_keys):
        pts = []
        for k in old_keys:
            st = self._st(k)
            if st["w"] is not None:
                pts.append(st["w"])
            pts.extend(st["r"])
        for k in new_keys:
            st = self._st(k)
            st["r"].extend(pts)

    def finalize(self):
        self.counts = {}
        self.totals = {k: 0 for k in self.semkeys}
        for e in ENGINES:
            cnt = {k: 0 for k in self.semkeys}
            for i, ent in enumerate(self.streams[e]):
                sk = self._semkey(e, i)
                if ent["inc"]:
                    cnt[sk] += 16 if ent["dma"] else 1
                self.counts[(e, i)] = (sk, cnt[sk])
            for k in self.semkeys:
                self.totals[k] = max(self.totals[k], cnt[k])

    def emit(self, eng_name, eng):
        cdma = 0
        dma_sk = (f"{eng_name}_dma"
                  if eng_name in ("gpsimd", "vector", "scalar")
                  else eng_name)
        for ent in self.streams[eng_name]:
            for we, wi in ent["waits"]:
                sk, val = self.counts[(we, wi)]
                if sk in self.relaxed:
                    val = self.totals[sk]
                eng.wait_ge(self.sems[sk], val)
            if ent["dma"] and not ent["relax"] and cdma > 0:
                eng.wait_ge(self.sems[dma_sk], cdma)
            inst = ent["fn"](eng)
            if ent["inc"]:
                if ent["relax"]:
                    sk = ent["relax"]
                elif ent["dma"]:
                    sk = dma_sk
                else:
                    sk = eng_name
                inst.then_inc(self.sems[sk], 16 if ent["dma"] else 1)
                if ent["dma"] and not ent["relax"]:
                    cdma += 16


class Alloc:
    def __init__(self, nc, base=16512, budget=229344):
        self.nc = nc
        self.off = base
        self.budget = budget

    def t(self, name, shape, dtype, at=None):
        per_part = int(np.prod(shape[1:])) * (2 if dtype == BF16 else 4)
        per_part = (per_part + 63) // 64 * 64
        if at is None:
            at = self.off
            self.off = at + per_part
        assert at + per_part <= self.budget, (
            f"SBUF overflow: {name} at {at}+{per_part} > {self.budget}")
        return self.nc.alloc_sbuf_tensor_at(name, list(shape), dtype,
                                            offset=at)


def build_nc(n_layers=L, with_ffn=True, dump=None):
    nc = bass.Bass("TRN2", target_bir_lowering=False, debug=False)

    x0 = nc.dram_tensor("x0", [T, C], F32, kind="ExternalInput").ap()
    idf_in = nc.dram_tensor("idf", [128, 128], F32, kind="ExternalInput").ap()
    wT = {}
    for l in range(L):
        for nm, shape in (("wk", [C, C]), ("wv", [C, C]), ("wr", [C, C]),
                          ("wo", [C, C]), ("wck", [C, F]), ("wcv", [F, C]),
                          ("wcr", [C, C])):
            wT[(nm, l)] = nc.dram_tensor(f"{nm}T{l}", shape, F32,
                                         kind="ExternalInput").ap()
    pv = {}
    for l in range(L):
        for nm in ("tmk", "tmv", "tmr", "cmk", "cmr", "td", "tf",
                   "ln1w", "ln1b", "ln2w", "ln2b"):
            pv[(nm, l)] = nc.dram_tensor(f"{nm}{l}", [128, NCB], F32,
                                         kind="ExternalInput").ap()
    ln0w_d = nc.dram_tensor("ln0w", [1, C], F32, kind="ExternalInput").ap()
    ln0b_d = nc.dram_tensor("ln0b", [1, C], F32, kind="ExternalInput").ap()
    w1_d = nc.dram_tensor("w1", [C, F], F32, kind="ExternalInput").ap()
    b1_d = nc.dram_tensor("b1", [128, NFB], F32, kind="ExternalInput").ap()
    w2_d = nc.dram_tensor("w2", [F, C], F32, kind="ExternalInput").ap()
    b2_d = nc.dram_tensor("b2", [1, C], F32, kind="ExternalInput").ap()
    out_d = nc.dram_tensor("out", [T, C], F32, kind="ExternalOutput").ap()
    bufA = nc.dram_tensor("xbufA", [T, C], F32, kind="Internal").ap()
    bufB = nc.dram_tensor("xbufB", [T, C], F32, kind="Internal").ap()
    wcv_bf = [nc.dram_tensor(f"wcv_bf{l}", [F, C], BF16,
                             kind="Internal").ap() for l in range(L)]
    w2_bf = nc.dram_tensor("w2_bf", [F, C], BF16, kind="Internal").ap()

    al = Alloc(nc)
    x_tm2 = [al.t("x_tmA", [128, SA // 128, C], F32),
             al.t("x_tmB", [128, SA // 128, C], F32)]
    out_scr = [al.t(f"out_scr{i}", [128, C], F32) for i in range(2)]
    xn = [al.t(f"xn{i}", [128, C], F32) for i in range(4)]
    h_cb = [al.t(f"h{i}", [128, SA + 1], BF16) for i in range(NCB)]
    xk_cb = [al.t(f"xk{i}", [128, SA], BF16) for i in range(NCB)]
    xr_cb = [al.t(f"xr{i}", [128, SA], BF16) for i in range(NCB)]
    ST1 = [al.t(f"st1_{n}", [128, 8], F32) for n in range(5)]
    ST2 = [al.t(f"st2_{n}", [128, 8], F32) for n in range(5)]
    cst = al.t("cst", [128, 3], F32)
    acar = al.t("acar", [128, NCB], F32)
    bcar = al.t("bcar", [128, NCB], F32)
    PNAMES = ("tmk", "tmv", "tmr", "cmk", "cmr", "ln1w", "ln1b",
              "ln2w", "ln2b", "ew", "eu")
    parbuf = al.t("parbuf", [128, L * len(PNAMES) * NCB], F32)
    par = {}
    for l in range(L):
        for i, nm in enumerate(PNAMES):
            j = (l * len(PNAMES) + i) * NCB
            par[(nm, l)] = parbuf[:, j:j + NCB]
    w0b = al.t("w0b", [128, C], F32)
    b0b = al.t("b0b", [128, C], F32)
    b1_sb = al.t("b1_sb", [128, NFB], F32)
    idf = al.t("idf_sb", [128, 128], F32)
    ones_col = al.t("ones_col", [1, 128], F32)

    rw1 = al.off
    wk_sb = al.t("wk_sb", [128, NCB, NCB, 128], BF16, at=rw1)
    wv_sb = al.t("wv_sb", [128, NCB, NCB, 128], BF16, at=rw1 + 16384)
    wr_sb = al.t("wr_sb", [128, NCB, NCB, 128], BF16, at=rw1 + 32768)
    wo_sb = al.t("wo_sb", [128, NCB, C], BF16, at=rw1 + 49152)
    wck_sb = al.t("wck_sb", [128, NFB, NCB, 128], BF16, at=rw1)
    al.off = rw1 + 65536
    RW1_A = [wk_sb, wv_sb, wr_sb, wo_sb]
    WCK_CHUNKS = [("wckc", c) for c in range(4)]

    rw2 = al.off
    o = rw2
    xv_cb = []
    for i in range(NCB):
        xv_cb.append(al.t(f"xv{i}", [128, SA], BF16, at=o)); o += 1024
    rwkvT = al.t("rwkvT", [128, NCB, SA], BF16, at=o); o += 8192
    ekP = al.t("ekP", [128, 2, SA], F32, at=o); o += 4096
    ekvP = al.t("ekvP", [128, 2, SA], F32, at=o); o += 4096
    mscP = al.t("mscP", [128, 2, SA], F32, at=o); o += 4096
    a_s, b_s = [], []
    for i in range(2):
        a_s.append(al.t(f"a{i}", [128, SA + 1], F32, at=o)); o += 2176
        b_s.append(al.t(f"b{i}", [128, SA + 1], F32, at=o)); o += 2176
    dscr = al.t("dscr", [128, 2, SA], BF16, at=o); o += 2048
    rw2_a_end = o
    RW2_A = (xv_cb + [rwkvT, ekP, ekvP, mscP, dscr] + a_s + b_s)
    o = rw2
    wcvb_s = []
    for i in range(2):
        wcvb_s.append(al.t(f"wcvb{i}", [128, 2, C], BF16, at=o)); o += 4096
    gj_s = []
    for i in range(2):
        gj_s.append(al.t(f"gj{i}", [128, C], F32, at=o)); o += 4096
    kfP = al.t("kfP", [128, 2, SB], BF16, at=o); o += 1024
    rlP = al.t("rlP", [128, 2, SB], F32, at=o); o += 2048
    xaux_tm = al.t("xaux", [128, SB // 128, C], F32, at=o)
    b2b = al.t("b2b", [128, C], F32, at=o)
    o += 8192
    wcr_sb = al.t("wcr_sb", [128, NCB, C], BF16, at=o)
    o += 16384
    rw2_b_end = o
    RW2_B = wcvb_s + gj_s + [kfP, rlP, xaux_tm, wcr_sb]
    al.off = rw2 + max(rw2_a_end - rw2, rw2_b_end - rw2)
    RW3_F = [b2b]

    sqjunk = out_scr[0]

    PT = nc.alloc_psum_tensor("PT", [128, 1024], F32)
    PF = nc.alloc_psum_tensor("PF", [128, 1024], F32)
    PR = nc.alloc_psum_tensor("PR", [128, 1024], F32)
    PO = nc.alloc_psum_tensor("PO", [128, 1024], F32)

    sch = Sched(nc)

    def V(fn, r=(), w=(), tag=""):
        sch.op("vector", fn, r, w, tag=tag)

    def S(fn, r=(), w=(), tag=""):
        sch.op("scalar", fn, r, w, tag=tag)

    def TE(fn, r=(), w=(), tag=""):
        sch.op("tensor", fn, r, w, tag=tag)

    def G(fn, r=(), w=(), tag=""):
        sch.op("gpsimd" if USE_POOL else "vector", fn, r, w, tag=tag)

    def DMAW(fn, r=(), w=(), tag=""):
        sch.op("gpsimd", fn, r, w, dma=True, tag=tag)

    def DMAX(fn, r=(), w=(), tag=""):
        sch.op("sync", fn, r, w, dma=True, tag=tag)

    def DMAS(fn, r=(), w=(), tag=""):
        sch.op("scalar", fn, r, w, dma=True, tag=tag)

    XD = {id(x0.tensor): "x0", id(bufA.tensor): "bufA",
          id(bufB.tensor): "bufB"}

    def dkey(ap):
        return XD.get(id(ap.tensor), ap.tensor)

    def CDMA(fn, w, tag):
        sch.op("sync", fn, writes=w, dma=True, relax="cst", tag=tag)

    def load_params(l):
        for nm in ("tmk", "tmv", "tmr", "cmk", "cmr", "ln1w", "ln1b",
                   "ln2w", "ln2b"):
            CDMA(lambda e, d=par[(nm, l)], s=pv[(nm, l)]: e.dma_start(
                d[:], s), w=[par[(nm, l)]], tag=f"p_{nm}{l}")
        CDMA(lambda e, d=par[("ew", l)], s=pv[("td", l)]: e.dma_start(
            d[:], s), w=[par[("ew", l)]], tag=f"p_td{l}")
        S(lambda e, p=par[("ew", l)]: e.activation(p[:], p[:], AF.Exp),
          r=[par[("ew", l)]], w=[par[("ew", l)]], tag=f"ew1_{l}")
        S(lambda e, p=par[("ew", l)]: e.activation(p[:], p[:], AF.Exp,
                                                   scale=-1.0),
          r=[par[("ew", l)]], w=[par[("ew", l)]], tag=f"ew2_{l}")
        CDMA(lambda e, d=par[("eu", l)], s=pv[("tf", l)]: e.dma_start(
            d[:], s), w=[par[("eu", l)]], tag=f"p_tf{l}")
        S(lambda e, p=par[("eu", l)]: e.activation(p[:], p[:], AF.Exp),
          r=[par[("eu", l)]], w=[par[("eu", l)]], tag=f"eu_{l}")
    CDMA(lambda e: e.dma_start(b1_sb[:], b1_d), w=[b1_sb], tag="b1")
    CDMA(lambda e: e.dma_start(idf[:], idf_in), w=[idf], tag="idf")
    V(lambda v: v.memset(ones_col[:], 1.0), w=[ones_col], tag="ones")
    V(lambda v: v.memset(cst[:, 0:1], EPS), w=[cst], tag="c_eps")
    V(lambda v: v.memset(cst[:, 1:2], 1.0), w=[cst], tag="c_one")
    V(lambda v: v.memset(cst[:, 2:3], EPS * EPS), w=[cst], tag="c_eps2")

    def bcast_row(src_row, dst, nm):
        DMAS(lambda e, s=src_row: e.dma_start(out_scr[1][0:1, :], s),
             w=[out_scr[1]], tag=f"row_{nm}")
        for half in range(2):
            TE(lambda t, half=half: t.matmul(
                PR[:, half * 512:(half + 1) * 512], ones_col[:],
                out_scr[1][0:1, half * 512:(half + 1) * 512],
                start=True, stop=True),
               r=[ones_col, out_scr[1]], w=[PR], tag=f"bc_{nm}{half}")
        V(lambda v, d=dst: v.tensor_copy(d[:], PR[:]),
          r=[PR], w=[dst], tag=f"bcc_{nm}")

    if n_layers > 0:
        bcast_row(ln0w_d, w0b, "w0b")
        bcast_row(ln0b_d, b0b, "b0b")

    def emit_dump_wcv(l):
        sch.op("gpsimd",
               lambda g, l=l: g.dma_start(wcv_bf[l][:, :], wT[("wcv", l)]),
               writes=[("wcv_bf", l)], dma=True, relax=f"dump{l}",
               tag=f"dump_wcv{l}")

    def emit_dump_w2():
        sch.op("gpsimd", lambda g: g.dma_start(w2_bf[:, :], w2_d),
               writes=[("w2_bf",)], dma=True, relax="dumpw2",
               tag="dump_w2")

    def load_weights_A(l):
        for wsb, key in ((wk_sb, "wk"), (wv_sb, "wv"), (wr_sb, "wr")):
            DMAW(lambda g, d=wsb, s=wT[(key, l)]: g.dma_start(
                d[:], s.rearrange("(a p) (b j) -> p a b j", p=128, j=128)),
                 w=[wsb], tag=f"ldA_{key}{l}")
        DMAW(lambda g, s=wT[("wo", l)]: g.dma_start(
            wo_sb[:], s.rearrange("(ob p) c -> p ob c", p=128)),
             w=[wo_sb], tag=f"ldA_wo{l}")

    def load_wck_chunked(src, tag):
        for c in range(4):
            for fb in range(8 * c, 8 * (c + 1)):
                DMAW(lambda g, fb=fb, s=src: g.dma_start(
                    wck_sb[:, fb, :, :],
                    s[:, 128 * fb:128 * (fb + 1)].rearrange(
                        "(a p) j -> p a j", p=128)),
                     w=[WCK_CHUNKS[c]], tag=f"{tag}_f{fb}")

    def load_weights_B(l):
        load_wck_chunked(wT[("wck", l)], f"ldB_wck{l}")
        DMAW(lambda g, s=wT[("wcr", l)]: g.dma_start(
            wcr_sb[:], s.rearrange("(ci p) c -> p ci c", p=128)),
             w=[wcr_sb], tag=f"ldB_wcr{l}")

    def load_weights_F():
        load_wck_chunked(w1_d, "ldF_w1")

    def stats_accum(tiles, keys, tbs, ST, tag):
        ssum, ssq = ST[0], ST[1]
        for tb in tbs:
            S(lambda e, a=tiles[tb], c=tb, ssq=ssq: e.activation(
                sqjunk[:], a, AF.Square, accum_out=ssq[:, c:c + 1]),
              r=[keys[tb]], w=[sqjunk, ssq], tag=f"{tag}_sq{tb}")
            S(lambda e, a=tiles[tb], c=tb, ssum=ssum: e.activation(
                sqjunk[:], a, AF.Identity, accum_out=ssum[:, c:c + 1]),
              r=[keys[tb]], w=[sqjunk, ssum], tag=f"{tag}_sm{tb}")

    def stats_ln(tiles, keys, ntb, ST, tag, folded=False, accum=True):
        ssum, ssq, smean, svar, srstd = ST
        if accum:
            stats_accum(tiles, keys, range(ntb), ST, tag)
        sl = slice(0, ntb)
        V(lambda v, ssum=ssum, smean=smean: v.tensor_scalar(
            smean[:, sl], ssum[:, sl], 1.0 / C, None, ALU.mult),
          r=[ssum], w=[smean], tag=f"{tag}_mean")
        V(lambda v, ssq=ssq, svar=svar: v.tensor_scalar(
            svar[:, sl], ssq[:, sl], 1.0 / C, None, ALU.mult),
          r=[ssq], w=[svar], tag=f"{tag}_v1")
        V(lambda v, ssq=ssq, smean=smean: v.tensor_mul(
            ssq[:, sl], smean[:, sl], smean[:, sl]),
          r=[smean], w=[ssq], tag=f"{tag}_msq")
        V(lambda v, ssq=ssq, svar=svar: v.tensor_sub(
            svar[:, sl], svar[:, sl], ssq[:, sl]),
          r=[ssq, svar], w=[svar], tag=f"{tag}_v2")
        if folded:
            S(lambda e, svar=svar: e.activation(
                svar[:, sl], svar[:, sl], AF.Ln, bias=cst[:, 2:3],
                scale=1.0 + EPS),
              r=[svar, cst], w=[svar], tag=f"{tag}_ln")
        else:
            S(lambda e, svar=svar: e.activation(svar[:, sl], svar[:, sl],
                                                AF.Ln, bias=cst[:, 0:1]),
              r=[svar, cst], w=[svar], tag=f"{tag}_ln")
        S(lambda e, svar=svar, srstd=srstd: e.activation(
            srstd[:, sl], svar[:, sl], AF.Exp, scale=-0.5),
          r=[svar], w=[srstd], tag=f"{tag}_ex")
        V(lambda v, smean=smean, srstd=srstd, ssum=ssum: v.tensor_mul(
            ssum[:, sl], smean[:, sl], srstd[:, sl]),
          r=[smean, srstd], w=[ssum], tag=f"{tag}_nm1")
        V(lambda v, ssum=ssum: v.tensor_scalar(
            ssum[:, sl], ssum[:, sl], -1.0, None, ALU.mult),
          r=[ssum], w=[ssum], tag=f"{tag}_nm2")

    def srckeys(src_name, t0, ntb):
        if src_name == "x0":
            return ["x0"]
        return [(src_name, t0 // 128 + i) for i in range(ntb)]

    def emit_prologue(tail_hook=None):
        ntb = SA // 128
        nstripes = T // SA
        hookL, hookM = tail_hook if tail_hook else (None, None)
        smean, srstd = ST1[2], ST1[4]
        for s in range(nstripes):
            if s == nstripes - 1 and hookL is not None:
                hookL()
            t0 = s * SA
            xm = x_tm2[s % 2]
            DMAS(lambda e, t0=t0, xm=xm: e.dma_start(
                xm[:], x0[t0:t0 + SA, :].rearrange(
                    "(tb p) c -> p tb c", p=128)),
                 r=["x0"], w=[xm], tag=f"P{s}_ldx")
            xt = [xm[:, tb, :] for tb in range(ntb)]
            xtk = [xm] * ntb
            stats_ln(xt, xtk, ntb, ST1, f"P{s}", folded=True)
            for tb in range(ntb):
                V(lambda v, tb=tb, xm=xm: v.tensor_scalar(
                    xm[:, tb, :], xm[:, tb, :],
                    smean[:, tb:tb + 1], srstd[:, tb:tb + 1],
                    ALU.subtract, ALU.mult),
                  r=[xm, smean, srstd], w=[xm], tag=f"P{s}_n{tb}")
                V(lambda v, tb=tb, xm=xm: v.tensor_mul(
                    xm[:, tb, :], xm[:, tb, :], w0b[:]),
                  r=[xm, w0b], w=[xm], tag=f"P{s}_w{tb}")
                V(lambda v, tb=tb, xm=xm: v.tensor_add(
                    xm[:, tb, :], xm[:, tb, :], b0b[:]),
                  r=[xm, b0b], w=[xm], tag=f"P{s}_b{tb}")
            DMAX(lambda e, t0=t0, xm=xm: e.dma_start(
                bufB[t0:t0 + SA, :].rearrange("(tb p) c -> p tb c", p=128),
                xm[:]),
                 r=[xm], w=[("bufB", t0 // 128 + tb) for tb in range(ntb)],
                 tag=f"P{s}_st")
        if hookM is not None:
            hookM()

    def make_phase_A(l, src, src_name):
        ntb = SA // 128
        smean, srstd = ST1[2], ST1[4]

        def head_load(s):
            t0 = s * SA
            xm = x_tm2[s % 2]
            DMAS(lambda e, t0=t0, src=src, xm=xm: e.dma_start(
                xm[:], src[t0:t0 + SA, :].rearrange(
                    "(tb p) c -> p tb c", p=128)),
                 r=srckeys(src_name, t0, ntb), w=[xm], tag=f"A{l}s{s}_ldx")

        def head_math(s, part=None):
            xm = x_tm2[s % 2]
            xt = [xm[:, tb, :] for tb in range(ntb)]
            xtk = [xm] * ntb
            if part in (None, 1):
                stats_accum(xt, xtk, range(0, 2), ST1, f"A{l}s{s}_l1")
            if part in (None, 2):
                stats_accum(xt, xtk, range(2, ntb), ST1, f"A{l}s{s}_l1")
                stats_ln(xt, xtk, ntb, ST1, f"A{l}s{s}_l1", accum=False)
            if part in (None, 3):
                for tb in range(ntb):
                    V(lambda v, tb=tb, xm=xm: v.tensor_scalar(
                        xn[tb][:], xm[:, tb, :],
                        smean[:, tb:tb + 1], srstd[:, tb:tb + 1],
                        ALU.subtract, ALU.mult),
                      r=[xm, smean, srstd], w=[xn[tb]],
                      tag=f"A{l}s{s}_xn{tb}")
        return head_load, head_math

    def phase_A(l, src, src_name, head, tail_hook=None, skip_head0=False):
        ntb = SA // 128
        nstripes = T // SA
        head_load, head_math = head
        hookL, hookM = tail_hook if tail_hook else (None, None)
        V(lambda v: v.memset(acar[:], 0.0), w=[acar], tag=f"A{l}_ac0")
        V(lambda v: v.memset(bcar[:], 0.0), w=[bcar], tag=f"A{l}_bc0")
        for cb in range(NCB):
            V(lambda v, cb=cb: v.memset(h_cb[cb][:, 0:1], 0.0),
              w=[h_cb[cb]], tag=f"A{l}_h0_{cb}")

        def prep(s):
            for cb in range(NCB):
                ptr = PT if cb % 4 < 2 else PF
                base = (cb % 2) * 512
                for tb in range(ntb):
                    TE(lambda t, tb=tb, cb=cb, base=base, ptr=ptr:
                       t.transpose(
                           ptr[:, base + tb * 128:base + (tb + 1) * 128],
                           xn[tb][:, cb * 128:(cb + 1) * 128], idf[:]),
                       r=[xn[tb], idf], w=[ptr], tag=f"A{l}s{s}_tr{cb}_{tb}")
                S(lambda e, cb=cb, base=base, ptr=ptr: e.activation(
                    h_cb[cb][:, 1:SA + 1], ptr[:, base:base + SA],
                    AF.Identity,
                    bias=par[("ln1b", l)][:, cb:cb + 1],
                    scale=par[("ln1w", l)][:, cb:cb + 1]),
                  r=[ptr, par[("ln1w", l)], par[("ln1b", l)]],
                  w=[h_cb[cb]], tag=f"A{l}s{s}_ev{cb}")
                dsc = dscr[:, cb % 2, :]
                G(lambda g, cb=cb, dsc=dsc: g.tensor_sub(
                    dsc, h_cb[cb][:, 1:SA + 1], h_cb[cb][:, 0:SA]),
                  r=[h_cb[cb]], w=[dscr], tag=f"A{l}s{s}_d{cb}")
                for nm, dst in (("tmk", xk_cb), ("tmv", xv_cb),
                                ("tmr", xr_cb)):
                    V(lambda v, cb=cb, nm=nm, dst=dst, dsc=dsc:
                      v.scalar_tensor_tensor(
                          dst[cb][:], dsc, par[(nm, l)][:, cb:cb + 1],
                          h_cb[cb][:, 0:SA], ALU.mult, ALU.add),
                      r=[dscr, h_cb[cb], par[(nm, l)]], w=[dst[cb]],
                      tag=f"A{l}s{s}_m{nm}{cb}")
                V(lambda v, cb=cb: v.tensor_copy(h_cb[cb][:, 0:1],
                                                 h_cb[cb][:, SA:SA + 1]),
                  r=[h_cb[cb]], w=[h_cb[cb]], tag=f"A{l}s{s}_bnd{cb}")

        if not skip_head0:
            head_load(0)
            head_math(0)
        prep(0)
        for s in range(nstripes):
            t0 = s * SA
            xm = x_tm2[s % 2]
            for p in range(4):
                if s + 1 < nstripes:
                    if p == 0:
                        head_load(s + 1)
                    elif p == 3:
                        head_math(s + 1, part=3)
                    else:
                        head_math(s + 1, part=p)
                elif p == 0 and hookL is not None:
                    hookL()
                cbs = (2 * p, 2 * p + 1)
                for j, cb in enumerate(cbs):
                    for ci in range(NCB):
                        TE(lambda t, cb=cb, ci=ci, j=j: t.matmul(
                            PF[:, j * 512:(j + 1) * 512],
                            wk_sb[:, ci, cb, :], xk_cb[ci][:],
                            start=(ci == 0), stop=(ci == NCB - 1)),
                           r=[wk_sb, xk_cb[ci]], w=[PF],
                           tag=f"A{l}s{s}_mmk{cb}_{ci}")
                for j, cb in enumerate(cbs):
                    for ci in range(NCB):
                        TE(lambda t, cb=cb, ci=ci, j=j: t.matmul(
                            PT[:, j * 512:(j + 1) * 512],
                            wv_sb[:, ci, cb, :], xv_cb[ci][:],
                            start=(ci == 0), stop=(ci == NCB - 1)),
                           r=[wv_sb, xv_cb[ci]], w=[PT],
                           tag=f"A{l}s{s}_mmv{cb}_{ci}")
                for j, cb in enumerate(cbs):
                    for ci in range(NCB):
                        TE(lambda t, cb=cb, ci=ci, j=j: t.matmul(
                            PR[:, j * 512:(j + 1) * 512],
                            wr_sb[:, ci, cb, :], xr_cb[ci][:],
                            start=(ci == 0), stop=(ci == NCB - 1)),
                           r=[wr_sb, xr_cb[ci]], w=[PR],
                           tag=f"A{l}s{s}_mmr{cb}_{ci}")
                S(lambda e: e.activation(
                    ekP[:].rearrange("p a b -> p (a b)"), PF[:], AF.Exp),
                  r=[PF], w=[ekP], tag=f"A{l}s{s}_ek{p}")
                S(lambda e: e.activation(
                    mscP[:].rearrange("p a b -> p (a b)"), PR[:], AF.Exp,
                    scale=-1.0),
                  r=[PR], w=[mscP], tag=f"A{l}s{s}_er{p}")
                V(lambda v: v.tensor_mul(
                    ekvP[:].rearrange("p a b -> p (a b)"),
                    ekP[:].rearrange("p a b -> p (a b)"), PT[:]),
                  r=[ekP, PT], w=[ekvP], tag=f"A{l}s{s}_ekv{p}")
                for j, cb in enumerate(cbs):
                    a_, b_ = a_s[j], b_s[j]
                    ew_ap = par[("ew", l)]
                    dec = AP(ew_ap.tensor, ew_ap.offset + cb,
                             [[L * len(PNAMES) * NCB, 128], [0, SA]])
                    V(lambda v, a_=a_, cb=cb: v.tensor_copy(
                        a_[:, 0:1], acar[:, cb:cb + 1]),
                      r=[acar], w=[a_], tag=f"A{l}s{s}_ac{cb}")
                    V(lambda v, a_=a_, dec=dec, j=j, cb=cb:
                      v.tensor_tensor_scan(a_[:, 1:SA + 1], dec,
                                           ekvP[:, j, :],
                                           acar[:, cb:cb + 1], ALU.mult,
                                           ALU.add),
                      r=[par[("ew", l)], ekvP, acar], w=[a_],
                      tag=f"A{l}s{s}_sa{cb}")
                    V(lambda v, a_=a_, cb=cb: v.tensor_copy(
                        acar[:, cb:cb + 1], a_[:, SA:SA + 1]),
                      r=[a_], w=[acar], tag=f"A{l}s{s}_ac2{cb}")
                    V(lambda v, b_=b_, cb=cb: v.tensor_copy(
                        b_[:, 0:1], bcar[:, cb:cb + 1]),
                      r=[bcar], w=[b_], tag=f"A{l}s{s}_bc{cb}")
                    V(lambda v, b_=b_, dec=dec, j=j, cb=cb:
                      v.tensor_tensor_scan(b_[:, 1:SA + 1], dec,
                                           ekP[:, j, :],
                                           bcar[:, cb:cb + 1], ALU.mult,
                                           ALU.add),
                      r=[par[("ew", l)], ekP, bcar], w=[b_],
                      tag=f"A{l}s{s}_sb{cb}")
                    V(lambda v, b_=b_, cb=cb: v.tensor_copy(
                        bcar[:, cb:cb + 1], b_[:, SA:SA + 1]),
                      r=[b_], w=[bcar], tag=f"A{l}s{s}_bc2{cb}")
                for j, cb in enumerate(cbs):
                    a_, b_ = a_s[j], b_s[j]
                    V(lambda v, a_=a_, j=j, cb=cb:
                      v.scalar_tensor_tensor(ekvP[:, j, :], ekvP[:, j, :],
                                             par[("eu", l)][:, cb:cb + 1],
                                             a_[:, 0:SA], ALU.mult,
                                             ALU.add),
                      r=[ekvP, a_, par[("eu", l)]], w=[ekvP],
                      tag=f"A{l}s{s}_num{cb}")
                    V(lambda v, b_=b_, j=j, cb=cb:
                      v.scalar_tensor_tensor(ekP[:, j, :], ekP[:, j, :],
                                             par[("eu", l)][:, cb:cb + 1],
                                             b_[:, 0:SA], ALU.mult,
                                             ALU.add),
                      r=[ekP, b_, par[("eu", l)]], w=[ekP],
                      tag=f"A{l}s{s}_den{cb}")
                V(lambda v: v.scalar_tensor_tensor(
                    ekP[:].rearrange("p a b -> p (a b)"),
                    mscP[:].rearrange("p a b -> p (a b)"), 1.0,
                    ekP[:].rearrange("p a b -> p (a b)"),
                    ALU.add, ALU.mult),
                  r=[mscP, ekP], w=[ekP], tag=f"A{l}s{s}_dx{p}")
                S(lambda e: e.activation(
                    ekP[:].rearrange("p a b -> p (a b)"),
                    ekP[:].rearrange("p a b -> p (a b)"), AF.Ln),
                  r=[ekP], w=[ekP], tag=f"A{l}s{s}_lnd{p}")
                S(lambda e: e.activation(
                    ekP[:].rearrange("p a b -> p (a b)"),
                    ekP[:].rearrange("p a b -> p (a b)"), AF.Exp,
                    scale=-1.0),
                  r=[ekP], w=[ekP], tag=f"A{l}s{s}_rec{p}")
                V(lambda v, p=p: v.tensor_mul(
                    rwkvT[:, 2 * p:2 * p + 2, :].rearrange(
                        "p a b -> p (a b)"),
                    ekvP[:].rearrange("p a b -> p (a b)"),
                    ekP[:].rearrange("p a b -> p (a b)")),
                  r=[ekvP, ekP], w=[rwkvT], tag=f"A{l}s{s}_rw{p}")
            if s + 1 < nstripes:
                prep(s + 1)
            elif hookM is not None:
                hookM()
            for tb in range(ntb):
                po = PO if tb % 2 == 0 else PR
                for ob in range(NCB):
                    for half in range(2):
                        TE(lambda t, tb=tb, ob=ob, half=half, po=po:
                           t.matmul(
                               po[:, half * 512:(half + 1) * 512],
                               rwkvT[:, ob, tb * 128:(tb + 1) * 128],
                               wo_sb[:, ob, half * 512:(half + 1) * 512],
                               start=(ob == 0), stop=(ob == NCB - 1)),
                           r=[rwkvT, wo_sb], w=[po],
                           tag=f"A{l}s{s}_wo{tb}_{ob}_{half}")
                osc = out_scr[tb % 2]
                V(lambda v, tb=tb, osc=osc, xm=xm, po=po: v.tensor_add(
                    osc[:], xm[:, tb, :], po[:]),
                  r=[xm, po], w=[osc], tag=f"A{l}s{s}_res{tb}")
                DMAX(lambda e, t0=t0, tb=tb, osc=osc: e.dma_start(
                    bufA[t0 + tb * 128:t0 + (tb + 1) * 128, :], osc[:]),
                     r=[osc], w=[("bufA", t0 // 128 + tb)],
                     tag=f"A{l}s{s}_st{tb}")

    def make_phase_B(l, src, src_name):
        ntb = SB // 128
        smean, srstd = ST1[2], ST1[4]

        def head_load(s):
            t0 = s * SB
            xm = x_tm2[s % 2]
            DMAS(lambda e, t0=t0, src=src, xm=xm: e.dma_start(
                xm[:, 0:ntb, :], src[t0:t0 + SB, :].rearrange(
                    "(tb p) c -> p tb c", p=128)),
                 r=srckeys(src_name, t0, ntb), w=[xm], tag=f"B{l}s{s}_ldx")

        def head_math(s):
            xm = x_tm2[s % 2]
            xt = [xm[:, tb, :] for tb in range(ntb)]
            xtk = [xm] * ntb
            stats_ln(xt, xtk, ntb, ST1, f"B{l}s{s}_l2")
            for tb in range(ntb):
                V(lambda v, tb=tb, xm=xm: v.tensor_scalar(
                    xn[tb][:], xm[:, tb, :],
                    smean[:, tb:tb + 1], srstd[:, tb:tb + 1],
                    ALU.subtract, ALU.mult),
                  r=[xm, smean, srstd], w=[xn[tb]], tag=f"B{l}s{s}_xn{tb}")
        return head_load, head_math

    def phase_B(l, src, src_name, dst_dram, dst_name, head, is_ffn=False,
                sandwich=False, tail_hook=None, skip_head0=False):
        ntb = SB // 128
        nstripes = T // SB
        head_load, head_math = head
        hookL, hookM = tail_hook if tail_hook else (None, None)
        if not is_ffn:
            for cb in range(NCB):
                V(lambda v, cb=cb: v.memset(h_cb[cb][:, 0:1], 0.0),
                  w=[h_cb[cb]], tag=f"B{l}_h0_{cb}")
        wstream = w2_bf if is_ffn else wcv_bf[l]
        wstream_key = ("w2_bf",) if is_ffn else ("wcv_bf", l)

        if not skip_head0:
            head_load(0)
            head_math(0)
        for s in range(nstripes):
            t0 = s * SB
            xm = x_tm2[s % 2]
            if s + 1 < nstripes:
                head_load(s + 1)
            elif hookL is not None:
                hookL()
            if sandwich:
                DMAS(lambda e, t0=t0: e.dma_start(
                    xaux_tm[:], x0[t0:t0 + SB, :].rearrange(
                        "(tb p) c -> p tb c", p=128)),
                     r=[dkey(x0)], w=[xaux_tm], tag=f"B{l}s{s}_ldx0")
            for cb in range(NCB):
                base = (cb % 4) * 256
                for tb in range(ntb):
                    TE(lambda t, tb=tb, cb=cb, base=base: t.transpose(
                        PT[:, base + tb * 128:base + (tb + 1) * 128],
                        xn[tb][:, cb * 128:(cb + 1) * 128], idf[:]),
                       r=[xn[tb], idf], w=[PT], tag=f"B{l}s{s}_tr{cb}_{tb}")
                if is_ffn:
                    S(lambda e, cb=cb, base=base: e.activation(
                        xk_cb[cb][:, 0:SB], PT[:, base:base + SB],
                        AF.Identity),
                      r=[PT], w=[xk_cb[cb]], tag=f"B{l}s{s}_ev{cb}")
                else:
                    S(lambda e, cb=cb, base=base: e.activation(
                        h_cb[cb][:, 1:SB + 1], PT[:, base:base + SB],
                        AF.Identity,
                        bias=par[("ln2b", l)][:, cb:cb + 1],
                        scale=par[("ln2w", l)][:, cb:cb + 1]),
                      r=[PT, par[("ln2w", l)], par[("ln2b", l)]],
                      w=[h_cb[cb]], tag=f"B{l}s{s}_ev{cb}")
                    dsc = rlP[:, cb % 2, :]
                    V(lambda v, cb=cb, dsc=dsc: v.tensor_sub(
                        dsc, h_cb[cb][:, 1:SB + 1],
                        h_cb[cb][:, 0:SB]),
                      r=[h_cb[cb]], w=[rlP], tag=f"B{l}s{s}_d{cb}")
                    for nm, dst in (("cmk", xk_cb), ("cmr", xr_cb)):
                        V(lambda v, cb=cb, nm=nm, dst=dst, dsc=dsc:
                          v.scalar_tensor_tensor(
                              dst[cb][:, 0:SB], dsc,
                              par[(nm, l)][:, cb:cb + 1],
                              h_cb[cb][:, 0:SB], ALU.mult, ALU.add),
                          r=[rlP, h_cb[cb], par[(nm, l)]], w=[dst[cb]],
                          tag=f"B{l}s{s}_m{nm}{cb}")
                    V(lambda v, cb=cb: v.tensor_copy(
                        h_cb[cb][:, 0:1], h_cb[cb][:, SB:SB + 1]),
                      r=[h_cb[cb]], w=[h_cb[cb]], tag=f"B{l}s{s}_bnd{cb}")
            if s + 1 < nstripes:
                head_math(s + 1)
            elif hookM is not None:
                hookM()
            for gp_ in range(NFB // 2):
                wb = wcvb_s[gp_ % 2]
                DMAW(lambda gq, gp_=gp_, wb=wb, ws=wstream: gq.dma_start(
                    wb[:], ws[gp_ * 256:(gp_ + 1) * 256, :].rearrange(
                        "(fb p) c -> p fb c", p=128)),
                     r=[wstream_key], w=[wb], tag=f"B{l}s{s}_wld{gp_}")
                if True:
                    for fj in range(2):
                        fb = gp_ * 2 + fj
                        for ci in range(NCB):
                            TE(lambda t, fb=fb, fj=fj, ci=ci: t.matmul(
                                PF[:, fj * 512:fj * 512 + SB],
                                wck_sb[:, fb, ci, :],
                                xk_cb[ci][:, 0:SB],
                                start=(ci == 0), stop=(ci == NCB - 1)),
                               r=[WCK_CHUNKS[fb // 8], xk_cb[ci]], w=[PF],
                               tag=f"B{l}s{s}_ck{fb}_{ci}")
                    if is_ffn:
                        for fj in range(2):
                            fb = gp_ * 2 + fj
                            S(lambda e, fb=fb, fj=fj: e.activation(
                                kfP[:, fj, :], PF[:, fj * 512:
                                                  fj * 512 + SB],
                                AF.Gelu, bias=b1_sb[:, fb:fb + 1]),
                              r=[PF, b1_sb], w=[kfP],
                              tag=f"B{l}s{s}_gel{fb}")
                    else:
                        pfv = AP(PF[:].tensor, 0, [[1024, 128], [512, 2],
                                                   [1, SB]])
                        V(lambda v, pfv=pfv: v.tensor_scalar_max(
                            rlP[:].rearrange("p a b -> p (a b)"), pfv,
                            0.0),
                          r=[PF], w=[rlP], tag=f"B{l}s{s}_rl{gp_}")
                        V(lambda v: v.tensor_mul(
                            kfP[:].rearrange("p a b -> p (a b)"),
                            rlP[:].rearrange("p a b -> p (a b)"),
                            rlP[:].rearrange("p a b -> p (a b)")),
                          r=[rlP], w=[kfP], tag=f"B{l}s{s}_sq{gp_}")
                    for fj in range(2):
                        fb = gp_ * 2 + fj
                        for tb in range(ntb):
                            pv = PR if tb == 0 else PO
                            for half in range(2):
                                TE(lambda t, fb=fb, fj=fj, tb=tb,
                                   half=half, wb=wb, pv=pv:
                                   t.matmul(
                                       pv[:, half * 512:(half + 1) * 512],
                                       kfP[:, fj, tb * 128:(tb + 1) * 128],
                                       wb[:, fj,
                                          half * 512:(half + 1) * 512],
                                       start=(fb == 0),
                                       stop=(fb == NFB - 1)),
                                   r=[kfP, wb], w=[pv],
                                   tag=f"B{l}s{s}_cv{fb}_{tb}_{half}")
            for tb in range(ntb):
                osc = out_scr[tb % 2]
                gj = gj_s[tb % 2]
                pv = PR if tb == 0 else PO
                if is_ffn:
                    V(lambda v, gj=gj, pv=pv: v.tensor_add(
                        gj[:], pv[:], b2b[:]),
                      r=[pv, b2b], w=[gj], tag=f"B{l}s{s}_b2{tb}")
                else:
                    for half in range(2):
                        for ci in range(NCB):
                            TE(lambda t, tb=tb, half=half, ci=ci: t.matmul(
                                PF[:, half * 512:(half + 1) * 512],
                                xr_cb[ci][:, tb * 128:(tb + 1) * 128],
                                wcr_sb[:, ci,
                                       half * 512:(half + 1) * 512],
                                start=(ci == 0), stop=(ci == NCB - 1)),
                               r=[xr_cb[ci], wcr_sb], w=[PF],
                               tag=f"B{l}s{s}_cr{tb}_{half}_{ci}")
                    S(lambda e, gj=gj: e.activation(
                        gj[:], PF[:], AF.Exp, scale=-1.0),
                      r=[PF], w=[gj], tag=f"B{l}s{s}_er{tb}")
                    S(lambda e, gj=gj: e.activation(gj[:], gj[:], AF.Ln,
                                                    bias=cst[:, 1:2]),
                      r=[gj, cst], w=[gj], tag=f"B{l}s{s}_l1p{tb}")
                    S(lambda e, gj=gj: e.activation(gj[:], gj[:], AF.Exp,
                                                    scale=-1.0),
                      r=[gj], w=[gj], tag=f"B{l}s{s}_sg{tb}")
                    V(lambda v, gj=gj, pv=pv: v.tensor_mul(
                        gj[:], gj[:], pv[:]),
                      r=[pv, gj], w=[gj], tag=f"B{l}s{s}_gkv{tb}")
                if sandwich:
                    V(lambda v, tb=tb, gj=gj, xm=xm: v.tensor_add(
                        gj[:], gj[:], xm[:, tb, :]),
                      r=[xm, gj], w=[gj], tag=f"B{l}s{s}_xb{tb}")
                elif is_ffn:
                    pass
                else:
                    V(lambda v, tb=tb, osc=osc, gj=gj, xm=xm: v.tensor_add(
                        osc[:], gj[:], xm[:, tb, :]),
                      r=[xm, gj], w=[osc], tag=f"B{l}s{s}_res{tb}")
                    DMAX(lambda e, t0=t0, tb=tb, osc=osc, d=dst_dram:
                         e.dma_start(
                             d[t0 + tb * 128:t0 + (tb + 1) * 128, :],
                             osc[:]),
                         r=[osc], w=[(dst_name, t0 // 128 + tb)],
                         tag=f"B{l}s{s}_st{tb}")
            if sandwich or is_ffn:
                gt = [gj_s[tb % 2][:] for tb in range(ntb)]
                gtk = [gj_s[tb % 2] for tb in range(ntb)]
                stats_ln(gt, gtk, ntb, ST2, f"B{l}s{s}_f")
                for tb in range(ntb):
                    osc = out_scr[tb % 2]
                    V(lambda v, tb=tb, osc=osc: v.tensor_scalar(
                        osc[:], gj_s[tb % 2][:], ST2[2][:, tb:tb + 1],
                        ST2[4][:, tb:tb + 1], ALU.subtract, ALU.mult),
                      r=[gj_s[tb % 2], ST2[2], ST2[4]], w=[osc],
                      tag=f"B{l}s{s}_fn{tb}")
                    if sandwich:
                        V(lambda v, tb=tb, osc=osc: v.tensor_add(
                            osc[:], osc[:], xaux_tm[:, tb, :]),
                          r=[xaux_tm, osc], w=[osc],
                          tag=f"B{l}s{s}_fa{tb}")
                    else:
                        V(lambda v, tb=tb, osc=osc, xm=xm: v.tensor_add(
                            osc[:], osc[:], xm[:, tb, :]),
                          r=[xm, osc], w=[osc], tag=f"B{l}s{s}_fa{tb}")
                    DMAX(lambda e, t0=t0, tb=tb, osc=osc, d=dst_dram:
                         e.dma_start(
                             d[t0 + tb * 128:t0 + (tb + 1) * 128, :],
                             osc[:]),
                         r=[osc], w=[(dst_name, t0 // 128 + tb)],
                         tag=f"B{l}s{s}_st{tb}")

    if n_layers > 0:
        load_weights_A(0)
    for l in range(n_layers):
        load_params(l)
    if n_layers > 0:
        emit_dump_wcv(0)

    use_ffn = with_ffn and n_layers == L
    specs = []
    cur, cur_name = (bufB, "bufB") if n_layers > 0 else (x0, "x0")
    for l in range(n_layers):
        def preA(l=l):
            for c in range(4):
                sch.alias([WCK_CHUNKS[c]], [RW1_A[c]])
            sch.alias(RW2_B, RW2_A)
            if l > 0:
                load_weights_A(l)
                emit_dump_wcv(l)
            if with_ffn and l == n_layers - 1:
                emit_dump_w2()
        hA = make_phase_A(l, cur, cur_name)
        specs.append((hA, preA,
                      lambda hook, skip, l=l, c=cur, cn=cur_name, h=hA:
                      phase_A(l, c, cn, h, tail_hook=hook,
                              skip_head0=skip)))

        def preB(l=l):
            for c in range(4):
                sch.alias([RW1_A[c]], [WCK_CHUNKS[c]])
            sch.alias(RW2_A, RW2_B)
            load_weights_B(l)
        sandwich = (l == L - 1) and use_ffn
        hB = make_phase_B(l, bufA, "bufA")
        specs.append((hB, preB,
                      lambda hook, skip, l=l, h=hB, sw=sandwich:
                      phase_B(l, bufA, "bufA", bufB, "bufB", h,
                              is_ffn=False, sandwich=sw, tail_hook=hook,
                              skip_head0=skip)))
        cur, cur_name = bufB, "bufB"
    if use_ffn:
        def preF():
            for c in range(4):
                sch.alias([RW1_A[c]], [WCK_CHUNKS[c]])
            sch.alias([xaux_tm], RW3_F)
            load_weights_F()
            DMAX(lambda e: e.dma_start(out_scr[1][0:1, :], b2_d),
                 w=[out_scr[1]], tag="row_b2")
            for half in range(2):
                TE(lambda t, half=half: t.matmul(
                    PR[:, half * 512:(half + 1) * 512], ones_col[:],
                    out_scr[1][0:1, half * 512:(half + 1) * 512],
                    start=True, stop=True),
                   r=[ones_col, out_scr[1]], w=[PR], tag=f"bc_b2{half}")
            V(lambda v: v.tensor_copy(b2b[:], PR[:]),
              r=[PR], w=[b2b], tag="bcc_b2")
        hF = make_phase_B(-1, bufB, "bufB")
        specs.append((hF, preF,
                      lambda hook, skip, h=hF:
                      phase_B(-1, bufB, "bufB", out_d, "out", h,
                              is_ffn=True, sandwich=False, tail_hook=hook,
                              skip_head0=skip)))

    def hook_pair(h):
        return (lambda: h[0](0), lambda: h[1](0))

    skip0 = False
    if n_layers > 0:
        emit_prologue(tail_hook=hook_pair(specs[0][0]))
        skip0 = True
    for i, (h, pre, body) in enumerate(specs):
        pre()
        nxt = specs[i + 1][0] if i + 1 < len(specs) else None
        body(hook_pair(nxt) if nxt is not None else None,
             skip0 if i == 0 else True)
    if not use_ffn:
        if dump == "A":
            cur, cur_name = bufA, "bufA"
        for s in range(T // SA):
            t0 = s * SA
            xm = x_tm2[s % 2]
            DMAX(lambda e, t0=t0, src=cur, xm=xm: e.dma_start(
                xm[:], src[t0:t0 + SA, :].rearrange(
                    "(tb p) c -> p tb c", p=128)),
                 r=srckeys(cur_name, t0, 4), w=[xm], tag=f"cp{s}_ld")
            DMAX(lambda e, t0=t0, xm=xm: e.dma_start(
                out_d[t0:t0 + SA, :].rearrange("(tb p) c -> p tb c", p=128),
                xm[:]),
                 r=[xm], w=[("out", s)], tag=f"cp{s}_st")

    sch.finalize()

    with nc.Block() as blk:
        @blk.tensor
        def _(eng):
            sch.emit("tensor", eng)

        @blk.vector
        def _(eng):
            sch.emit("vector", eng)

        @blk.scalar
        def _(eng):
            sch.emit("scalar", eng)

        @blk.gpsimd
        def _(eng):
            sch.emit("gpsimd", eng)

        @blk.sync
        def _(eng):
            sch.emit("sync", eng)
            for sk in (["sync", "gpsimd_dma", "vector_dma", "scalar_dma"]
                       + sorted(sch.relaxed)):
                if sch.totals[sk] > 0:
                    eng.wait_ge(sch.sems[sk], sch.totals[sk])

    return nc


def make_inputs(inputs, n_layers=L):
    def row128(v):
        return np.ascontiguousarray(np.asarray(v).reshape(NCB, 128).T,
                                    dtype=np.float32)

    base = {
        "idf": np.eye(128, dtype=np.float32),
        "ln0w": np.ascontiguousarray(
            np.asarray(inputs["L0_attn_ln0_w"], dtype=np.float32)[None, :]),
        "ln0b": np.ascontiguousarray(
            np.asarray(inputs["L0_attn_ln0_b"], dtype=np.float32)[None, :]),
        "w1": np.ascontiguousarray(inputs["L0_ffn_w1"], dtype=np.float32),
        "b1": np.ascontiguousarray(
            np.asarray(inputs["L0_ffn_b1"],
                       dtype=np.float32).reshape(NFB, 128).T),
        "w2": np.ascontiguousarray(inputs["L0_ffn_w2"], dtype=np.float32),
        "b2": np.ascontiguousarray(
            np.asarray(inputs["L0_ffn_b2"], dtype=np.float32)[None, :]),
    }
    for l in range(L):
        base[f"wkT{l}"] = np.ascontiguousarray(
            np.asarray(inputs["L0_attn_Wk"][l], dtype=np.float32).T)
        base[f"wvT{l}"] = np.ascontiguousarray(
            np.asarray(inputs["L0_attn_Wv"][l], dtype=np.float32).T)
        base[f"wrT{l}"] = np.ascontiguousarray(
            np.asarray(inputs["L0_attn_Wr"][l], dtype=np.float32).T)
        base[f"woT{l}"] = np.ascontiguousarray(
            np.asarray(inputs["L0_attn_Wo"][l], dtype=np.float32).T)
        base[f"wckT{l}"] = np.ascontiguousarray(
            np.asarray(inputs["L0_attn_Wck"][l], dtype=np.float32).T)
        base[f"wcvT{l}"] = np.ascontiguousarray(
            np.asarray(inputs["L0_attn_Wcv"][l], dtype=np.float32).T)
        base[f"wcrT{l}"] = np.ascontiguousarray(
            np.asarray(inputs["L0_attn_Wcr"][l], dtype=np.float32).T)
        base[f"tmk{l}"] = row128(inputs["L0_attn_tmk"][l])
        base[f"tmv{l}"] = row128(inputs["L0_attn_tmv"][l])
        base[f"tmr{l}"] = row128(inputs["L0_attn_tmr"][l])
        base[f"cmk{l}"] = row128(inputs["L0_attn_cmk"][l])
        base[f"cmr{l}"] = row128(inputs["L0_attn_cmr"][l])
        base[f"td{l}"] = row128(inputs["L0_attn_time_decay"][l])
        base[f"tf{l}"] = row128(inputs["L0_attn_time_first"][l])
        base[f"ln1w{l}"] = row128(inputs["L0_attn_ln1_w"][l])
        base[f"ln1b{l}"] = row128(inputs["L0_attn_ln1_b"][l])
        base[f"ln2w{l}"] = row128(inputs["L0_attn_ln2_w"][l])
        base[f"ln2b{l}"] = row128(inputs["L0_attn_ln2_b"][l])
    return base


def kernel(**inputs):
    n = 8
    x = np.ascontiguousarray(inputs["L0_attn_x"], dtype=np.float32)
    base = make_inputs(inputs)
    nc = build_nc()
    in_maps = []
    for b in range(n):
        m = dict(base)
        m["x0"] = np.ascontiguousarray(x[b])
        in_maps.append(m)
    res = run_bass_kernel_spmd(nc, in_maps, core_ids=list(range(n)))
    return np.stack([r["out"] for r in res.results], axis=0)


# version 18
# speedup vs baseline: 1.0265x; 1.0265x over previous
import numpy as np
import concourse.bass as bass
import concourse.mybir as mybir
from concourse.ap import AP
from concourse.bass_utils import run_bass_kernel_spmd

F32 = mybir.dt.float32
BF16 = mybir.dt.bfloat16
AF = mybir.ActivationFunctionType
ALU = mybir.AluOpType
AX = mybir.AxisListType.X

T, C, F, L = 4096, 1024, 4096, 4
NCB = C // 128
NFB = F // 128
SA, SB = 512, 256
EPS = 1e-5

ENGINES = ("tensor", "vector", "scalar", "gpsimd", "sync")
USE_POOL = True
REPHASE = True
FOLD_PROLOGUE = True


class Sched:

    SELF_SYNC = ("vector", "scalar", "gpsimd")

    def __init__(self, nc):
        self.nc = nc
        self.streams = {e: [] for e in ENGINES}
        self.semkeys = set(ENGINES) | {"gpsimd_dma", "vector_dma",
                                       "scalar_dma"}
        self.sems = {e: nc.alloc_semaphore(f"sem_{e}") for e in self.semkeys}
        self.state = {}
        self.hw = {}
        self.relaxed = set()

    def _sem(self, key):
        if key not in self.sems:
            self.semkeys.add(key)
            self.sems[key] = self.nc.alloc_semaphore(f"sem_{key}")
        return self.sems[key]

    @staticmethod
    def _key(b):
        return b if isinstance(b, (str, tuple, int)) else id(b)

    def _st(self, key):
        return self.state.setdefault(self._key(key), {"w": None, "r": []})

    def _semkey(self, eng, idx):
        ent = self.streams[eng][idx]
        if ent.get("relax"):
            return ent["relax"]
        if ent["dma"] and eng in ("gpsimd", "vector", "scalar"):
            return f"{eng}_dma"
        return eng

    def op(self, eng, fn, reads=(), writes=(), dma=False, relax=None,
           tag=""):
        if relax is not None:
            self._sem(relax)
            self.relaxed.add(relax)
        idx = len(self.streams[eng])
        selfsync = eng in self.SELF_SYNC
        waits = {}

        def need(we, wi):
            if we == eng and not selfsync:
                return
            sk = self._semkey(we, wi)
            k = (we, sk)
            waits[k] = max(waits.get(k, -1), wi)

        for b in reads:
            st = self._st(b)
            if st["w"] is not None:
                need(*st["w"])
        for b in writes:
            st = self._st(b)
            if st["w"] is not None:
                need(*st["w"])
            for re_, ri in st["r"]:
                need(re_, ri)
        final_waits = []
        for (we, sk), wi in waits.items():
            hk = (eng, sk)
            if self.hw.get(hk, -1) >= wi:
                continue
            self.hw[hk] = wi
            self.streams[we][wi]["inc"] = True
            final_waits.append((we, wi))
        self.streams[eng].append(
            {"fn": fn, "waits": final_waits, "inc": dma, "dma": dma,
             "relax": relax, "tag": tag})
        for b in writes:
            st = self._st(b)
            st["w"] = (eng, idx)
            st["r"] = []
        for b in reads:
            self._st(b)["r"].append((eng, idx))
        return idx

    def alias(self, old_keys, new_keys):
        pts = []
        for k in old_keys:
            st = self._st(k)
            if st["w"] is not None:
                pts.append(st["w"])
            pts.extend(st["r"])
        for k in new_keys:
            st = self._st(k)
            st["r"].extend(pts)

    def finalize(self):
        self.counts = {}
        self.totals = {k: 0 for k in self.semkeys}
        for e in ENGINES:
            cnt = {k: 0 for k in self.semkeys}
            for i, ent in enumerate(self.streams[e]):
                sk = self._semkey(e, i)
                if ent["inc"]:
                    cnt[sk] += 16 if ent["dma"] else 1
                self.counts[(e, i)] = (sk, cnt[sk])
            for k in self.semkeys:
                self.totals[k] = max(self.totals[k], cnt[k])

    def emit(self, eng_name, eng):
        cdma = 0
        rcount = {}
        dma_sk = (f"{eng_name}_dma"
                  if eng_name in ("gpsimd", "vector", "scalar")
                  else eng_name)
        for ent in self.streams[eng_name]:
            for we, wi in ent["waits"]:
                sk, val = self.counts[(we, wi)]
                if sk in self.relaxed:
                    val = self.totals[sk]
                eng.wait_ge(self.sems[sk], val)
            if ent["dma"] and not ent["relax"] and cdma > 0:
                eng.wait_ge(self.sems[dma_sk], cdma)
            if ent["relax"]:
                i = rcount.get(ent["relax"], 0)
                rcount[ent["relax"]] = i + 1
                if i >= 4:
                    eng.wait_ge(self.sems[ent["relax"]], 16 * (i - 3))
            inst = ent["fn"](eng)
            if ent["inc"]:
                if ent["relax"]:
                    sk = ent["relax"]
                elif ent["dma"]:
                    sk = dma_sk
                else:
                    sk = eng_name
                inst.then_inc(self.sems[sk], 16 if ent["dma"] else 1)
                if ent["dma"] and not ent["relax"]:
                    cdma += 16


class Alloc:
    def __init__(self, nc, base=16512, budget=229344):
        self.nc = nc
        self.off = base
        self.budget = budget

    def t(self, name, shape, dtype, at=None):
        per_part = int(np.prod(shape[1:])) * (2 if dtype == BF16 else 4)
        per_part = (per_part + 63) // 64 * 64
        if at is None:
            at = self.off
            self.off = at + per_part
        assert at + per_part <= self.budget, (
            f"SBUF overflow: {name} at {at}+{per_part} > {self.budget}")
        return self.nc.alloc_sbuf_tensor_at(name, list(shape), dtype,
                                            offset=at)


def build_nc(n_layers=L, with_ffn=True, dump=None):
    nc = bass.Bass("TRN2", target_bir_lowering=False, debug=False)

    x0 = nc.dram_tensor("x0", [T, C], F32, kind="ExternalInput").ap()
    idf_in = nc.dram_tensor("idf", [128, 128], F32, kind="ExternalInput").ap()
    wT = {}
    for l in range(L):
        for nm, shape in (("wk", [C, C]), ("wv", [C, C]), ("wr", [C, C]),
                          ("wo", [C, C]), ("wck", [C, F]), ("wcv", [F, C]),
                          ("wcr", [C, C])):
            wT[(nm, l)] = nc.dram_tensor(f"{nm}T{l}", shape, F32,
                                         kind="ExternalInput").ap()
    pv = {}
    for l in range(L):
        for nm in ("tmk", "tmv", "tmr", "cmk", "cmr", "td", "tf",
                   "ln1w", "ln1b", "ln2w", "ln2b"):
            pv[(nm, l)] = nc.dram_tensor(f"{nm}{l}", [128, NCB], F32,
                                         kind="ExternalInput").ap()
    ln0w_d = nc.dram_tensor("ln0w", [1, C], F32, kind="ExternalInput").ap()
    ln0b_d = nc.dram_tensor("ln0b", [1, C], F32, kind="ExternalInput").ap()
    w1_d = nc.dram_tensor("w1", [C, F], F32, kind="ExternalInput").ap()
    b1_d = nc.dram_tensor("b1", [128, NFB], F32, kind="ExternalInput").ap()
    w2_d = nc.dram_tensor("w2", [F, C], F32, kind="ExternalInput").ap()
    b2_d = nc.dram_tensor("b2", [1, C], F32, kind="ExternalInput").ap()
    out_d = nc.dram_tensor("out", [T, C], F32, kind="ExternalOutput").ap()
    bufA = nc.dram_tensor("xbufA", [T, C], F32, kind="Internal").ap()
    bufB = nc.dram_tensor("xbufB", [T, C], F32, kind="Internal").ap()
    wcv_bf = [nc.dram_tensor(f"wcv_bf{l}", [F, C], BF16,
                             kind="Internal").ap() for l in range(L)]
    w2_bf = nc.dram_tensor("w2_bf", [F, C], BF16, kind="Internal").ap()

    al = Alloc(nc)
    x_tm2 = [al.t("x_tmA", [128, SA // 128, C], F32),
             al.t("x_tmB", [128, SA // 128, C], F32)]
    out_scr = [al.t(f"out_scr{i}", [128, C], F32) for i in range(2)]
    xn = [al.t(f"xn{i}", [128, C], F32) for i in range(4)]
    h_cb = [al.t(f"h{i}", [128, SA + 1], BF16) for i in range(NCB)]
    xk_cb = [al.t(f"xk{i}", [128, SA], BF16) for i in range(NCB)]
    xr_cb = [al.t(f"xr{i}", [128, SA], BF16) for i in range(NCB)]
    ST1 = [al.t(f"st1_{n}", [128, 8], F32) for n in range(5)]
    ST2 = [al.t(f"st2_{n}", [128, 8], F32) for n in range(5)]
    cst = al.t("cst", [128, 3], F32)
    acar = al.t("acar", [128, NCB], F32)
    bcar = al.t("bcar", [128, NCB], F32)
    PNAMES = ("tmk", "tmv", "tmr", "cmk", "cmr", "ln1w", "ln1b",
              "ln2w", "ln2b", "ew", "eu")
    parbuf = al.t("parbuf", [128, L * len(PNAMES) * NCB], F32)
    par = {}
    for l in range(L):
        for i, nm in enumerate(PNAMES):
            j = (l * len(PNAMES) + i) * NCB
            par[(nm, l)] = parbuf[:, j:j + NCB]
    w0b = al.t("w0b", [128, C], F32)
    b0b = al.t("b0b", [128, C], F32)
    b1_sb = al.t("b1_sb", [128, NFB], F32)
    idf = al.t("idf_sb", [128, 128], F32)
    ones_col = al.t("ones_col", [1, 128], F32)

    rw1 = al.off
    wk_sb = al.t("wk_sb", [128, NCB, NCB, 128], BF16, at=rw1)
    wv_sb = al.t("wv_sb", [128, NCB, NCB, 128], BF16, at=rw1 + 16384)
    wr_sb = al.t("wr_sb", [128, NCB, NCB, 128], BF16, at=rw1 + 32768)
    wo_sb = al.t("wo_sb", [128, NCB, C], BF16, at=rw1 + 49152)
    wck_sb = al.t("wck_sb", [128, NCB, NFB, 128], BF16, at=rw1)
    al.off = rw1 + 65536
    RW1_A = [wk_sb, wv_sb, wr_sb, wo_sb]
    RW1_B = [wck_sb]

    rw2 = al.off
    o = rw2
    xv_cb = []
    for i in range(NCB):
        xv_cb.append(al.t(f"xv{i}", [128, SA], BF16, at=o)); o += 1024
    rwkvT = al.t("rwkvT", [128, NCB, SA], BF16, at=o); o += 8192
    ekP = al.t("ekP", [128, 2, SA], F32, at=o); o += 4096
    ekvP = al.t("ekvP", [128, 2, SA], F32, at=o); o += 4096
    mscP = al.t("mscP", [128, 2, SA], F32, at=o); o += 4096
    a_s, b_s = [], []
    for i in range(2):
        a_s.append(al.t(f"a{i}", [128, SA + 1], F32, at=o)); o += 2176
        b_s.append(al.t(f"b{i}", [128, SA + 1], F32, at=o)); o += 2176
    dscr = al.t("dscr", [128, 2, SA], BF16, at=o); o += 2048
    rw2_a_end = o
    RW2_A = (xv_cb + [rwkvT, ekP, ekvP, mscP, dscr] + a_s + b_s)
    o = rw2
    wcvb_s = []
    for i in range(2):
        wcvb_s.append(al.t(f"wcvb{i}", [128, 2, C], BF16, at=o)); o += 4096
    gj_s = []
    for i in range(2):
        gj_s.append(al.t(f"gj{i}", [128, C], F32, at=o)); o += 4096
    kfP = al.t("kfP", [128, 2, SB], BF16, at=o); o += 1024
    rlP = al.t("rlP", [128, 2, SB], F32, at=o); o += 2048
    xaux_tm = al.t("xaux", [128, SB // 128, C], F32, at=o)
    b2b = al.t("b2b", [128, C], F32, at=o)
    o += 8192
    wcr_sb = al.t("wcr_sb", [128, NCB, C], BF16, at=o)
    o += 16384
    rw2_b_end = o
    RW2_B = wcvb_s + gj_s + [kfP, rlP, xaux_tm, wcr_sb]
    al.off = rw2 + max(rw2_a_end - rw2, rw2_b_end - rw2)
    RW3_F = [b2b]

    sqjunk = out_scr[0]

    PT = nc.alloc_psum_tensor("PT", [128, 1024], F32)
    PF = nc.alloc_psum_tensor("PF", [128, 1024], F32)
    PR = nc.alloc_psum_tensor("PR", [128, 1024], F32)
    PO = nc.alloc_psum_tensor("PO", [128, 1024], F32)

    sch = Sched(nc)

    def V(fn, r=(), w=(), tag=""):
        sch.op("vector", fn, r, w, tag=tag)

    def S(fn, r=(), w=(), tag=""):
        sch.op("scalar", fn, r, w, tag=tag)

    def TE(fn, r=(), w=(), tag=""):
        sch.op("tensor", fn, r, w, tag=tag)

    def G(fn, r=(), w=(), tag=""):
        sch.op("gpsimd" if USE_POOL else "vector", fn, r, w, tag=tag)

    def DMAW(fn, r=(), w=(), tag=""):
        sch.op("gpsimd", fn, r, w, dma=True, tag=tag)

    def DMAX(fn, r=(), w=(), tag=""):
        sch.op("sync", fn, r, w, dma=True, tag=tag)

    def DMAS(fn, r=(), w=(), tag=""):
        sch.op("scalar", fn, r, w, dma=True, tag=tag)

    XD = {id(x0.tensor): "x0", id(bufA.tensor): "bufA",
          id(bufB.tensor): "bufB"}

    def dkey(ap):
        return XD.get(id(ap.tensor), ap.tensor)

    def CDMA(fn, w, tag):
        sch.op("sync", fn, writes=w, dma=True, relax="cst", tag=tag)

    def load_params(l):
        for nm in ("tmk", "tmv", "tmr", "cmk", "cmr", "ln1w", "ln1b",
                   "ln2w", "ln2b"):
            CDMA(lambda e, d=par[(nm, l)], s=pv[(nm, l)]: e.dma_start(
                d[:], s), w=[par[(nm, l)]], tag=f"p_{nm}{l}")
        CDMA(lambda e, d=par[("ew", l)], s=pv[("td", l)]: e.dma_start(
            d[:], s), w=[par[("ew", l)]], tag=f"p_td{l}")
        S(lambda e, p=par[("ew", l)]: e.activation(p[:], p[:], AF.Exp),
          r=[par[("ew", l)]], w=[par[("ew", l)]], tag=f"ew1_{l}")
        S(lambda e, p=par[("ew", l)]: e.activation(p[:], p[:], AF.Exp,
                                                   scale=-1.0),
          r=[par[("ew", l)]], w=[par[("ew", l)]], tag=f"ew2_{l}")
        CDMA(lambda e, d=par[("eu", l)], s=pv[("tf", l)]: e.dma_start(
            d[:], s), w=[par[("eu", l)]], tag=f"p_tf{l}")
        S(lambda e, p=par[("eu", l)]: e.activation(p[:], p[:], AF.Exp),
          r=[par[("eu", l)]], w=[par[("eu", l)]], tag=f"eu_{l}")
    CDMA(lambda e: e.dma_start(b1_sb[:], b1_d), w=[b1_sb], tag="b1")
    CDMA(lambda e: e.dma_start(idf[:], idf_in), w=[idf], tag="idf")
    V(lambda v: v.memset(ones_col[:], 1.0), w=[ones_col], tag="ones")
    V(lambda v: v.memset(cst[:, 0:1], EPS), w=[cst], tag="c_eps")
    V(lambda v: v.memset(cst[:, 1:2], 1.0), w=[cst], tag="c_one")
    V(lambda v: v.memset(cst[:, 2:3], EPS * EPS), w=[cst], tag="c_eps2")

    def bcast_row(src_row, dst, nm):
        DMAS(lambda e, s=src_row: e.dma_start(out_scr[1][0:1, :], s),
             w=[out_scr[1]], tag=f"row_{nm}")
        for half in range(2):
            TE(lambda t, half=half: t.matmul(
                PR[:, half * 512:(half + 1) * 512], ones_col[:],
                out_scr[1][0:1, half * 512:(half + 1) * 512],
                start=True, stop=True),
               r=[ones_col, out_scr[1]], w=[PR], tag=f"bc_{nm}{half}")
        V(lambda v, d=dst: v.tensor_copy(d[:], PR[:]),
          r=[PR], w=[dst], tag=f"bcc_{nm}")

    if n_layers > 0:
        bcast_row(ln0w_d, w0b, "w0b")
        bcast_row(ln0b_d, b0b, "b0b")

    def emit_dump_wcv(l):
        sch.op("gpsimd",
               lambda g, l=l: g.dma_start(wcv_bf[l][:, :], wT[("wcv", l)]),
               writes=[("wcv_bf", l)], dma=True, relax=f"dump{l}",
               tag=f"dump_wcv{l}")

    def emit_dump_w2():
        sch.op("gpsimd", lambda g: g.dma_start(w2_bf[:, :], w2_d),
               writes=[("w2_bf",)], dma=True, relax="dumpw2",
               tag="dump_w2")

    def load_weights_A(l):
        for wsb, key in ((wk_sb, "wk"), (wv_sb, "wv"), (wr_sb, "wr")):
            DMAW(lambda g, d=wsb, s=wT[(key, l)]: g.dma_start(
                d[:], s.rearrange("(a p) (b j) -> p a b j", p=128, j=128)),
                 w=[wsb], tag=f"ldA_{key}{l}")
        DMAW(lambda g, s=wT[("wo", l)]: g.dma_start(
            wo_sb[:], s.rearrange("(ob p) c -> p ob c", p=128)),
             w=[wo_sb], tag=f"ldA_wo{l}")

    def load_weights_B(l):
        DMAW(lambda g, s=wT[("wck", l)]: g.dma_start(
            wck_sb[:], s.rearrange("(a p) (b j) -> p a b j", p=128, j=128)),
             w=[wck_sb], tag=f"ldB_wck{l}")
        DMAW(lambda g, s=wT[("wcr", l)]: g.dma_start(
            wcr_sb[:], s.rearrange("(ci p) c -> p ci c", p=128)),
             w=[wcr_sb], tag=f"ldB_wcr{l}")

    def load_weights_F():
        DMAW(lambda g: g.dma_start(
            wck_sb[:], w1_d.rearrange("(a p) (b j) -> p a b j", p=128,
                                      j=128)),
             w=[wck_sb], tag="ldF_w1")

    def stats_accum(tiles, keys, tbs, ST, tag):
        ssum, ssq = ST[0], ST[1]
        for tb in tbs:
            S(lambda e, a=tiles[tb], c=tb, ssq=ssq: e.activation(
                sqjunk[:], a, AF.Square, accum_out=ssq[:, c:c + 1]),
              r=[keys[tb]], w=[sqjunk, ssq], tag=f"{tag}_sq{tb}")
            S(lambda e, a=tiles[tb], c=tb, ssum=ssum: e.activation(
                sqjunk[:], a, AF.Identity, accum_out=ssum[:, c:c + 1]),
              r=[keys[tb]], w=[sqjunk, ssum], tag=f"{tag}_sm{tb}")

    def stats_ln(tiles, keys, ntb, ST, tag, folded=False, accum=True):
        ssum, ssq, smean, svar, srstd = ST
        if accum:
            stats_accum(tiles, keys, range(ntb), ST, tag)
        sl = slice(0, ntb)
        V(lambda v, ssum=ssum, smean=smean: v.tensor_scalar(
            smean[:, sl], ssum[:, sl], 1.0 / C, None, ALU.mult),
          r=[ssum], w=[smean], tag=f"{tag}_mean")
        V(lambda v, ssq=ssq, svar=svar: v.tensor_scalar(
            svar[:, sl], ssq[:, sl], 1.0 / C, None, ALU.mult),
          r=[ssq], w=[svar], tag=f"{tag}_v1")
        V(lambda v, ssq=ssq, smean=smean: v.tensor_mul(
            ssq[:, sl], smean[:, sl], smean[:, sl]),
          r=[smean], w=[ssq], tag=f"{tag}_msq")
        V(lambda v, ssq=ssq, svar=svar: v.tensor_sub(
            svar[:, sl], svar[:, sl], ssq[:, sl]),
          r=[ssq, svar], w=[svar], tag=f"{tag}_v2")
        if folded:
            S(lambda e, svar=svar: e.activation(
                svar[:, sl], svar[:, sl], AF.Ln, bias=cst[:, 2:3],
                scale=1.0 + EPS),
              r=[svar, cst], w=[svar], tag=f"{tag}_ln")
        else:
            S(lambda e, svar=svar: e.activation(svar[:, sl], svar[:, sl],
                                                AF.Ln, bias=cst[:, 0:1]),
              r=[svar, cst], w=[svar], tag=f"{tag}_ln")
        S(lambda e, svar=svar, srstd=srstd: e.activation(
            srstd[:, sl], svar[:, sl], AF.Exp, scale=-0.5),
          r=[svar], w=[srstd], tag=f"{tag}_ex")
        V(lambda v, smean=smean, srstd=srstd, ssum=ssum: v.tensor_mul(
            ssum[:, sl], smean[:, sl], srstd[:, sl]),
          r=[smean, srstd], w=[ssum], tag=f"{tag}_nm1")
        V(lambda v, ssum=ssum: v.tensor_scalar(
            ssum[:, sl], ssum[:, sl], -1.0, None, ALU.mult),
          r=[ssum], w=[ssum], tag=f"{tag}_nm2")

    def srckeys(src_name, t0, ntb):
        if src_name == "x0":
            return ["x0"]
        return [(src_name, t0 // 128 + i) for i in range(ntb)]

    def emit_prologue(tail_hook=None):
        ntb = SA // 128
        nstripes = T // SA
        hookL, hookM = tail_hook if tail_hook else (None, None)
        smean, srstd = ST1[2], ST1[4]
        for s in range(nstripes):
            if s == nstripes - 1 and hookL is not None:
                hookL()
            t0 = s * SA
            xm = x_tm2[s % 2]
            DMAS(lambda e, t0=t0, xm=xm: e.dma_start(
                xm[:], x0[t0:t0 + SA, :].rearrange(
                    "(tb p) c -> p tb c", p=128)),
                 r=["x0"], w=[xm], tag=f"P{s}_ldx")
            xt = [xm[:, tb, :] for tb in range(ntb)]
            xtk = [xm] * ntb
            stats_ln(xt, xtk, ntb, ST1, f"P{s}", folded=True)
            for tb in range(ntb):
                V(lambda v, tb=tb, xm=xm: v.tensor_scalar(
                    xm[:, tb, :], xm[:, tb, :],
                    smean[:, tb:tb + 1], srstd[:, tb:tb + 1],
                    ALU.subtract, ALU.mult),
                  r=[xm, smean, srstd], w=[xm], tag=f"P{s}_n{tb}")
                V(lambda v, tb=tb, xm=xm: v.tensor_mul(
                    xm[:, tb, :], xm[:, tb, :], w0b[:]),
                  r=[xm, w0b], w=[xm], tag=f"P{s}_w{tb}")
                V(lambda v, tb=tb, xm=xm: v.tensor_add(
                    xm[:, tb, :], xm[:, tb, :], b0b[:]),
                  r=[xm, b0b], w=[xm], tag=f"P{s}_b{tb}")
            DMAX(lambda e, t0=t0, xm=xm: e.dma_start(
                bufB[t0:t0 + SA, :].rearrange("(tb p) c -> p tb c", p=128),
                xm[:]),
                 r=[xm], w=[("bufB", t0 // 128 + tb) for tb in range(ntb)],
                 tag=f"P{s}_st")
        if hookM is not None:
            hookM()

    def make_phase_A(l, src, src_name):
        ntb = SA // 128
        smean, srstd = ST1[2], ST1[4]

        def head_load(s):
            t0 = s * SA
            xm = x_tm2[s % 2]
            DMAS(lambda e, t0=t0, src=src, xm=xm: e.dma_start(
                xm[:], src[t0:t0 + SA, :].rearrange(
                    "(tb p) c -> p tb c", p=128)),
                 r=srckeys(src_name, t0, ntb), w=[xm], tag=f"A{l}s{s}_ldx")

        def head_math(s, part=None):
            xm = x_tm2[s % 2]
            xt = [xm[:, tb, :] for tb in range(ntb)]
            xtk = [xm] * ntb
            if part in (None, 1):
                stats_accum(xt, xtk, range(0, 2), ST1, f"A{l}s{s}_l1")
            if part in (None, 2):
                stats_accum(xt, xtk, range(2, ntb), ST1, f"A{l}s{s}_l1")
                stats_ln(xt, xtk, ntb, ST1, f"A{l}s{s}_l1", accum=False)
            if part in (None, 3):
                for tb in range(ntb):
                    V(lambda v, tb=tb, xm=xm: v.tensor_scalar(
                        xn[tb][:], xm[:, tb, :],
                        smean[:, tb:tb + 1], srstd[:, tb:tb + 1],
                        ALU.subtract, ALU.mult),
                      r=[xm, smean, srstd], w=[xn[tb]],
                      tag=f"A{l}s{s}_xn{tb}")
        return head_load, head_math

    def phase_A(l, src, src_name, head, tail_hook=None, skip_head0=False):
        ntb = SA // 128
        nstripes = T // SA
        head_load, head_math = head
        hookL, hookM = tail_hook if tail_hook else (None, None)
        V(lambda v: v.memset(acar[:], 0.0), w=[acar], tag=f"A{l}_ac0")
        V(lambda v: v.memset(bcar[:], 0.0), w=[bcar], tag=f"A{l}_bc0")
        for cb in range(NCB):
            V(lambda v, cb=cb: v.memset(h_cb[cb][:, 0:1], 0.0),
              w=[h_cb[cb]], tag=f"A{l}_h0_{cb}")

        def prep(s):
            for cb in range(NCB):
                ptr = PT if cb % 4 < 2 else PF
                base = (cb % 2) * 512
                for tb in range(ntb):
                    TE(lambda t, tb=tb, cb=cb, base=base, ptr=ptr:
                       t.transpose(
                           ptr[:, base + tb * 128:base + (tb + 1) * 128],
                           xn[tb][:, cb * 128:(cb + 1) * 128], idf[:]),
                       r=[xn[tb], idf], w=[ptr], tag=f"A{l}s{s}_tr{cb}_{tb}")
                S(lambda e, cb=cb, base=base, ptr=ptr: e.activation(
                    h_cb[cb][:, 1:SA + 1], ptr[:, base:base + SA],
                    AF.Identity,
                    bias=par[("ln1b", l)][:, cb:cb + 1],
                    scale=par[("ln1w", l)][:, cb:cb + 1]),
                  r=[ptr, par[("ln1w", l)], par[("ln1b", l)]],
                  w=[h_cb[cb]], tag=f"A{l}s{s}_ev{cb}")
                dsc = dscr[:, cb % 2, :]
                G(lambda g, cb=cb, dsc=dsc: g.tensor_sub(
                    dsc, h_cb[cb][:, 1:SA + 1], h_cb[cb][:, 0:SA]),
                  r=[h_cb[cb]], w=[dscr], tag=f"A{l}s{s}_d{cb}")
                for nm, dst in (("tmk", xk_cb), ("tmv", xv_cb),
                                ("tmr", xr_cb)):
                    V(lambda v, cb=cb, nm=nm, dst=dst, dsc=dsc:
                      v.scalar_tensor_tensor(
                          dst[cb][:], dsc, par[(nm, l)][:, cb:cb + 1],
                          h_cb[cb][:, 0:SA], ALU.mult, ALU.add),
                      r=[dscr, h_cb[cb], par[(nm, l)]], w=[dst[cb]],
                      tag=f"A{l}s{s}_m{nm}{cb}")
                V(lambda v, cb=cb: v.tensor_copy(h_cb[cb][:, 0:1],
                                                 h_cb[cb][:, SA:SA + 1]),
                  r=[h_cb[cb]], w=[h_cb[cb]], tag=f"A{l}s{s}_bnd{cb}")

        if not skip_head0:
            head_load(0)
            head_math(0)
        prep(0)
        for s in range(nstripes):
            t0 = s * SA
            xm = x_tm2[s % 2]
            for p in range(4):
                if s + 1 < nstripes:
                    if p == 0:
                        head_load(s + 1)
                    elif p == 3:
                        head_math(s + 1, part=3)
                    else:
                        head_math(s + 1, part=p)
                elif p == 0 and hookL is not None:
                    hookL()
                cbs = (2 * p, 2 * p + 1)
                for j, cb in enumerate(cbs):
                    for ci in range(NCB):
                        TE(lambda t, cb=cb, ci=ci, j=j: t.matmul(
                            PF[:, j * 512:(j + 1) * 512],
                            wk_sb[:, ci, cb, :], xk_cb[ci][:],
                            start=(ci == 0), stop=(ci == NCB - 1)),
                           r=[wk_sb, xk_cb[ci]], w=[PF],
                           tag=f"A{l}s{s}_mmk{cb}_{ci}")
                for j, cb in enumerate(cbs):
                    for ci in range(NCB):
                        TE(lambda t, cb=cb, ci=ci, j=j: t.matmul(
                            PT[:, j * 512:(j + 1) * 512],
                            wv_sb[:, ci, cb, :], xv_cb[ci][:],
                            start=(ci == 0), stop=(ci == NCB - 1)),
                           r=[wv_sb, xv_cb[ci]], w=[PT],
                           tag=f"A{l}s{s}_mmv{cb}_{ci}")
                for j, cb in enumerate(cbs):
                    for ci in range(NCB):
                        TE(lambda t, cb=cb, ci=ci, j=j: t.matmul(
                            PR[:, j * 512:(j + 1) * 512],
                            wr_sb[:, ci, cb, :], xr_cb[ci][:],
                            start=(ci == 0), stop=(ci == NCB - 1)),
                           r=[wr_sb, xr_cb[ci]], w=[PR],
                           tag=f"A{l}s{s}_mmr{cb}_{ci}")
                S(lambda e: e.activation(
                    ekP[:].rearrange("p a b -> p (a b)"), PF[:], AF.Exp),
                  r=[PF], w=[ekP], tag=f"A{l}s{s}_ek{p}")
                S(lambda e: e.activation(
                    mscP[:].rearrange("p a b -> p (a b)"), PR[:], AF.Exp,
                    scale=-1.0),
                  r=[PR], w=[mscP], tag=f"A{l}s{s}_er{p}")
                V(lambda v: v.tensor_mul(
                    ekvP[:].rearrange("p a b -> p (a b)"),
                    ekP[:].rearrange("p a b -> p (a b)"), PT[:]),
                  r=[ekP, PT], w=[ekvP], tag=f"A{l}s{s}_ekv{p}")
                for j, cb in enumerate(cbs):
                    a_, b_ = a_s[j], b_s[j]
                    ew_ap = par[("ew", l)]
                    dec = AP(ew_ap.tensor, ew_ap.offset + cb,
                             [[L * len(PNAMES) * NCB, 128], [0, SA]])
                    V(lambda v, a_=a_, cb=cb: v.tensor_copy(
                        a_[:, 0:1], acar[:, cb:cb + 1]),
                      r=[acar], w=[a_], tag=f"A{l}s{s}_ac{cb}")
                    V(lambda v, a_=a_, dec=dec, j=j, cb=cb:
                      v.tensor_tensor_scan(a_[:, 1:SA + 1], dec,
                                           ekvP[:, j, :],
                                           acar[:, cb:cb + 1], ALU.mult,
                                           ALU.add),
                      r=[par[("ew", l)], ekvP, acar], w=[a_],
                      tag=f"A{l}s{s}_sa{cb}")
                    V(lambda v, a_=a_, cb=cb: v.tensor_copy(
                        acar[:, cb:cb + 1], a_[:, SA:SA + 1]),
                      r=[a_], w=[acar], tag=f"A{l}s{s}_ac2{cb}")
                    V(lambda v, b_=b_, cb=cb: v.tensor_copy(
                        b_[:, 0:1], bcar[:, cb:cb + 1]),
                      r=[bcar], w=[b_], tag=f"A{l}s{s}_bc{cb}")
                    V(lambda v, b_=b_, dec=dec, j=j, cb=cb:
                      v.tensor_tensor_scan(b_[:, 1:SA + 1], dec,
                                           ekP[:, j, :],
                                           bcar[:, cb:cb + 1], ALU.mult,
                                           ALU.add),
                      r=[par[("ew", l)], ekP, bcar], w=[b_],
                      tag=f"A{l}s{s}_sb{cb}")
                    V(lambda v, b_=b_, cb=cb: v.tensor_copy(
                        bcar[:, cb:cb + 1], b_[:, SA:SA + 1]),
                      r=[b_], w=[bcar], tag=f"A{l}s{s}_bc2{cb}")
                for j, cb in enumerate(cbs):
                    a_, b_ = a_s[j], b_s[j]
                    V(lambda v, a_=a_, j=j, cb=cb:
                      v.scalar_tensor_tensor(ekvP[:, j, :], ekvP[:, j, :],
                                             par[("eu", l)][:, cb:cb + 1],
                                             a_[:, 0:SA], ALU.mult,
                                             ALU.add),
                      r=[ekvP, a_, par[("eu", l)]], w=[ekvP],
                      tag=f"A{l}s{s}_num{cb}")
                    V(lambda v, b_=b_, j=j, cb=cb:
                      v.scalar_tensor_tensor(ekP[:, j, :], ekP[:, j, :],
                                             par[("eu", l)][:, cb:cb + 1],
                                             b_[:, 0:SA], ALU.mult,
                                             ALU.add),
                      r=[ekP, b_, par[("eu", l)]], w=[ekP],
                      tag=f"A{l}s{s}_den{cb}")
                V(lambda v: v.scalar_tensor_tensor(
                    ekP[:].rearrange("p a b -> p (a b)"),
                    mscP[:].rearrange("p a b -> p (a b)"), 1.0,
                    ekP[:].rearrange("p a b -> p (a b)"),
                    ALU.add, ALU.mult),
                  r=[mscP, ekP], w=[ekP], tag=f"A{l}s{s}_dx{p}")
                S(lambda e: e.activation(
                    ekP[:].rearrange("p a b -> p (a b)"),
                    ekP[:].rearrange("p a b -> p (a b)"), AF.Ln),
                  r=[ekP], w=[ekP], tag=f"A{l}s{s}_lnd{p}")
                S(lambda e: e.activation(
                    ekP[:].rearrange("p a b -> p (a b)"),
                    ekP[:].rearrange("p a b -> p (a b)"), AF.Exp,
                    scale=-1.0),
                  r=[ekP], w=[ekP], tag=f"A{l}s{s}_rec{p}")
                V(lambda v, p=p: v.tensor_mul(
                    rwkvT[:, 2 * p:2 * p + 2, :].rearrange(
                        "p a b -> p (a b)"),
                    ekvP[:].rearrange("p a b -> p (a b)"),
                    ekP[:].rearrange("p a b -> p (a b)")),
                  r=[ekvP, ekP], w=[rwkvT], tag=f"A{l}s{s}_rw{p}")
            if s + 1 < nstripes:
                prep(s + 1)
            elif hookM is not None:
                hookM()
            for tb in range(ntb):
                po = PO if tb % 2 == 0 else PR
                for ob in range(NCB):
                    for half in range(2):
                        TE(lambda t, tb=tb, ob=ob, half=half, po=po:
                           t.matmul(
                               po[:, half * 512:(half + 1) * 512],
                               rwkvT[:, ob, tb * 128:(tb + 1) * 128],
                               wo_sb[:, ob, half * 512:(half + 1) * 512],
                               start=(ob == 0), stop=(ob == NCB - 1)),
                           r=[rwkvT, wo_sb], w=[po],
                           tag=f"A{l}s{s}_wo{tb}_{ob}_{half}")
                osc = out_scr[tb % 2]
                V(lambda v, tb=tb, osc=osc, xm=xm, po=po: v.tensor_add(
                    osc[:], xm[:, tb, :], po[:]),
                  r=[xm, po], w=[osc], tag=f"A{l}s{s}_res{tb}")
                DMAX(lambda e, t0=t0, tb=tb, osc=osc: e.dma_start(
                    bufA[t0 + tb * 128:t0 + (tb + 1) * 128, :], osc[:]),
                     r=[osc], w=[("bufA", t0 // 128 + tb)],
                     tag=f"A{l}s{s}_st{tb}")

    def make_phase_B(l, src, src_name):
        ntb = SB // 128
        smean, srstd = ST1[2], ST1[4]

        def head_load(s):
            t0 = s * SB
            xm = x_tm2[s % 2]
            DMAS(lambda e, t0=t0, src=src, xm=xm: e.dma_start(
                xm[:, 0:ntb, :], src[t0:t0 + SB, :].rearrange(
                    "(tb p) c -> p tb c", p=128)),
                 r=srckeys(src_name, t0, ntb), w=[xm], tag=f"B{l}s{s}_ldx")

        def head_math(s):
            xm = x_tm2[s % 2]
            xt = [xm[:, tb, :] for tb in range(ntb)]
            xtk = [xm] * ntb
            stats_ln(xt, xtk, ntb, ST1, f"B{l}s{s}_l2")
            for tb in range(ntb):
                V(lambda v, tb=tb, xm=xm: v.tensor_scalar(
                    xn[tb][:], xm[:, tb, :],
                    smean[:, tb:tb + 1], srstd[:, tb:tb + 1],
                    ALU.subtract, ALU.mult),
                  r=[xm, smean, srstd], w=[xn[tb]], tag=f"B{l}s{s}_xn{tb}")
        return head_load, head_math

    def phase_B(l, src, src_name, dst_dram, dst_name, head, is_ffn=False,
                sandwich=False, tail_hook=None, skip_head0=False):
        ntb = SB // 128
        nstripes = T // SB
        head_load, head_math = head
        hookL, hookM = tail_hook if tail_hook else (None, None)
        if not is_ffn:
            for cb in range(NCB):
                V(lambda v, cb=cb: v.memset(h_cb[cb][:, 0:1], 0.0),
                  w=[h_cb[cb]], tag=f"B{l}_h0_{cb}")
        wstream = w2_bf if is_ffn else wcv_bf[l]
        wstream_key = ("w2_bf",) if is_ffn else ("wcv_bf", l)

        if not skip_head0:
            head_load(0)
            head_math(0)
        for s in range(nstripes):
            t0 = s * SB
            xm = x_tm2[s % 2]
            if s + 1 < nstripes:
                head_load(s + 1)
            elif hookL is not None:
                hookL()
            if sandwich:
                DMAS(lambda e, t0=t0: e.dma_start(
                    xaux_tm[:], x0[t0:t0 + SB, :].rearrange(
                        "(tb p) c -> p tb c", p=128)),
                     r=[dkey(x0)], w=[xaux_tm], tag=f"B{l}s{s}_ldx0")
            for cb in range(NCB):
                base = (cb % 4) * 256
                for tb in range(ntb):
                    TE(lambda t, tb=tb, cb=cb, base=base: t.transpose(
                        PT[:, base + tb * 128:base + (tb + 1) * 128],
                        xn[tb][:, cb * 128:(cb + 1) * 128], idf[:]),
                       r=[xn[tb], idf], w=[PT], tag=f"B{l}s{s}_tr{cb}_{tb}")
                if is_ffn:
                    S(lambda e, cb=cb, base=base: e.activation(
                        xk_cb[cb][:, 0:SB], PT[:, base:base + SB],
                        AF.Identity),
                      r=[PT], w=[xk_cb[cb]], tag=f"B{l}s{s}_ev{cb}")
                else:
                    S(lambda e, cb=cb, base=base: e.activation(
                        h_cb[cb][:, 1:SB + 1], PT[:, base:base + SB],
                        AF.Identity,
                        bias=par[("ln2b", l)][:, cb:cb + 1],
                        scale=par[("ln2w", l)][:, cb:cb + 1]),
                      r=[PT, par[("ln2w", l)], par[("ln2b", l)]],
                      w=[h_cb[cb]], tag=f"B{l}s{s}_ev{cb}")
                    dsc = rlP[:, cb % 2, :]
                    V(lambda v, cb=cb, dsc=dsc: v.tensor_sub(
                        dsc, h_cb[cb][:, 1:SB + 1],
                        h_cb[cb][:, 0:SB]),
                      r=[h_cb[cb]], w=[rlP], tag=f"B{l}s{s}_d{cb}")
                    for nm, dst in (("cmk", xk_cb), ("cmr", xr_cb)):
                        V(lambda v, cb=cb, nm=nm, dst=dst, dsc=dsc:
                          v.scalar_tensor_tensor(
                              dst[cb][:, 0:SB], dsc,
                              par[(nm, l)][:, cb:cb + 1],
                              h_cb[cb][:, 0:SB], ALU.mult, ALU.add),
                          r=[rlP, h_cb[cb], par[(nm, l)]], w=[dst[cb]],
                          tag=f"B{l}s{s}_m{nm}{cb}")
                    V(lambda v, cb=cb: v.tensor_copy(
                        h_cb[cb][:, 0:1], h_cb[cb][:, SB:SB + 1]),
                      r=[h_cb[cb]], w=[h_cb[cb]], tag=f"B{l}s{s}_bnd{cb}")
            if s + 1 < nstripes:
                head_math(s + 1)
            elif hookM is not None:
                hookM()
            for gp_ in range(NFB // 2):
                wb = wcvb_s[gp_ % 2]
                DMAW(lambda gq, gp_=gp_, wb=wb, ws=wstream: gq.dma_start(
                    wb[:], ws[gp_ * 256:(gp_ + 1) * 256, :].rearrange(
                        "(fb p) c -> p fb c", p=128)),
                     r=[wstream_key], w=[wb], tag=f"B{l}s{s}_wld{gp_}")
                if True:
                    for fj in range(2):
                        fb = gp_ * 2 + fj
                        for ci in range(NCB):
                            TE(lambda t, fb=fb, fj=fj, ci=ci: t.matmul(
                                PF[:, fj * 512:fj * 512 + SB],
                                wck_sb[:, ci, fb, :],
                                xk_cb[ci][:, 0:SB],
                                start=(ci == 0), stop=(ci == NCB - 1)),
                               r=[wck_sb, xk_cb[ci]], w=[PF],
                               tag=f"B{l}s{s}_ck{fb}_{ci}")
                    if is_ffn:
                        for fj in range(2):
                            fb = gp_ * 2 + fj
                            S(lambda e, fb=fb, fj=fj: e.activation(
                                kfP[:, fj, :], PF[:, fj * 512:
                                                  fj * 512 + SB],
                                AF.Gelu, bias=b1_sb[:, fb:fb + 1]),
                              r=[PF, b1_sb], w=[kfP],
                              tag=f"B{l}s{s}_gel{fb}")
                    else:
                        pfv = AP(PF[:].tensor, 0, [[1024, 128], [512, 2],
                                                   [1, SB]])
                        V(lambda v, pfv=pfv: v.tensor_scalar_max(
                            rlP[:].rearrange("p a b -> p (a b)"), pfv,
                            0.0),
                          r=[PF], w=[rlP], tag=f"B{l}s{s}_rl{gp_}")
                        V(lambda v: v.tensor_mul(
                            kfP[:].rearrange("p a b -> p (a b)"),
                            rlP[:].rearrange("p a b -> p (a b)"),
                            rlP[:].rearrange("p a b -> p (a b)")),
                          r=[rlP], w=[kfP], tag=f"B{l}s{s}_sq{gp_}")
                    for fj in range(2):
                        fb = gp_ * 2 + fj
                        for tb in range(ntb):
                            pv = PR if tb == 0 else PO
                            for half in range(2):
                                TE(lambda t, fb=fb, fj=fj, tb=tb,
                                   half=half, wb=wb, pv=pv:
                                   t.matmul(
                                       pv[:, half * 512:(half + 1) * 512],
                                       kfP[:, fj, tb * 128:(tb + 1) * 128],
                                       wb[:, fj,
                                          half * 512:(half + 1) * 512],
                                       start=(fb == 0),
                                       stop=(fb == NFB - 1)),
                                   r=[kfP, wb], w=[pv],
                                   tag=f"B{l}s{s}_cv{fb}_{tb}_{half}")
            for tb in range(ntb):
                osc = out_scr[tb % 2]
                gj = gj_s[tb % 2]
                pv = PR if tb == 0 else PO
                if is_ffn:
                    V(lambda v, gj=gj, pv=pv: v.tensor_add(
                        gj[:], pv[:], b2b[:]),
                      r=[pv, b2b], w=[gj], tag=f"B{l}s{s}_b2{tb}")
                else:
                    for half in range(2):
                        for ci in range(NCB):
                            TE(lambda t, tb=tb, half=half, ci=ci: t.matmul(
                                PF[:, half * 512:(half + 1) * 512],
                                xr_cb[ci][:, tb * 128:(tb + 1) * 128],
                                wcr_sb[:, ci,
                                       half * 512:(half + 1) * 512],
                                start=(ci == 0), stop=(ci == NCB - 1)),
                               r=[xr_cb[ci], wcr_sb], w=[PF],
                               tag=f"B{l}s{s}_cr{tb}_{half}_{ci}")
                    S(lambda e, gj=gj: e.activation(
                        gj[:], PF[:], AF.Exp, scale=-1.0),
                      r=[PF], w=[gj], tag=f"B{l}s{s}_er{tb}")
                    S(lambda e, gj=gj: e.activation(gj[:], gj[:], AF.Ln,
                                                    bias=cst[:, 1:2]),
                      r=[gj, cst], w=[gj], tag=f"B{l}s{s}_l1p{tb}")
                    S(lambda e, gj=gj: e.activation(gj[:], gj[:], AF.Exp,
                                                    scale=-1.0),
                      r=[gj], w=[gj], tag=f"B{l}s{s}_sg{tb}")
                    V(lambda v, gj=gj, pv=pv: v.tensor_mul(
                        gj[:], gj[:], pv[:]),
                      r=[pv, gj], w=[gj], tag=f"B{l}s{s}_gkv{tb}")
                if sandwich:
                    V(lambda v, tb=tb, gj=gj, xm=xm: v.tensor_add(
                        gj[:], gj[:], xm[:, tb, :]),
                      r=[xm, gj], w=[gj], tag=f"B{l}s{s}_xb{tb}")
                elif is_ffn:
                    pass
                else:
                    V(lambda v, tb=tb, osc=osc, gj=gj, xm=xm: v.tensor_add(
                        osc[:], gj[:], xm[:, tb, :]),
                      r=[xm, gj], w=[osc], tag=f"B{l}s{s}_res{tb}")
                    DMAX(lambda e, t0=t0, tb=tb, osc=osc, d=dst_dram:
                         e.dma_start(
                             d[t0 + tb * 128:t0 + (tb + 1) * 128, :],
                             osc[:]),
                         r=[osc], w=[(dst_name, t0 // 128 + tb)],
                         tag=f"B{l}s{s}_st{tb}")
            if sandwich or is_ffn:
                gt = [gj_s[tb % 2][:] for tb in range(ntb)]
                gtk = [gj_s[tb % 2] for tb in range(ntb)]
                stats_ln(gt, gtk, ntb, ST2, f"B{l}s{s}_f")
                for tb in range(ntb):
                    osc = out_scr[tb % 2]
                    V(lambda v, tb=tb, osc=osc: v.tensor_scalar(
                        osc[:], gj_s[tb % 2][:], ST2[2][:, tb:tb + 1],
                        ST2[4][:, tb:tb + 1], ALU.subtract, ALU.mult),
                      r=[gj_s[tb % 2], ST2[2], ST2[4]], w=[osc],
                      tag=f"B{l}s{s}_fn{tb}")
                    if sandwich:
                        V(lambda v, tb=tb, osc=osc: v.tensor_add(
                            osc[:], osc[:], xaux_tm[:, tb, :]),
                          r=[xaux_tm, osc], w=[osc],
                          tag=f"B{l}s{s}_fa{tb}")
                    else:
                        V(lambda v, tb=tb, osc=osc, xm=xm: v.tensor_add(
                            osc[:], osc[:], xm[:, tb, :]),
                          r=[xm, osc], w=[osc], tag=f"B{l}s{s}_fa{tb}")
                    DMAX(lambda e, t0=t0, tb=tb, osc=osc, d=dst_dram:
                         e.dma_start(
                             d[t0 + tb * 128:t0 + (tb + 1) * 128, :],
                             osc[:]),
                         r=[osc], w=[(dst_name, t0 // 128 + tb)],
                         tag=f"B{l}s{s}_st{tb}")

    if n_layers > 0:
        load_weights_A(0)
    for l in range(n_layers):
        load_params(l)
    if n_layers > 0:
        emit_dump_wcv(0)

    use_ffn = with_ffn and n_layers == L
    specs = []
    cur, cur_name = (bufB, "bufB") if n_layers > 0 else (x0, "x0")
    for l in range(n_layers):
        def preA(l=l):
            sch.alias(RW1_B, RW1_A)
            sch.alias(RW2_B, RW2_A)
            if l > 0:
                load_weights_A(l)
                emit_dump_wcv(l)
            if with_ffn and l == n_layers - 1:
                emit_dump_w2()
        hA = make_phase_A(l, cur, cur_name)
        specs.append((hA, preA,
                      lambda hook, skip, l=l, c=cur, cn=cur_name, h=hA:
                      phase_A(l, c, cn, h, tail_hook=hook,
                              skip_head0=skip)))

        def preB(l=l):
            sch.alias(RW1_A, RW1_B)
            sch.alias(RW2_A, RW2_B)
            load_weights_B(l)
        sandwich = (l == L - 1) and use_ffn
        hB = make_phase_B(l, bufA, "bufA")
        specs.append((hB, preB,
                      lambda hook, skip, l=l, h=hB, sw=sandwich:
                      phase_B(l, bufA, "bufA", bufB, "bufB", h,
                              is_ffn=False, sandwich=sw, tail_hook=hook,
                              skip_head0=skip)))
        cur, cur_name = bufB, "bufB"
    if use_ffn:
        def preF():
            sch.alias(RW1_A, RW1_B)
            sch.alias([xaux_tm], RW3_F)
            load_weights_F()
            DMAX(lambda e: e.dma_start(out_scr[1][0:1, :], b2_d),
                 w=[out_scr[1]], tag="row_b2")
            for half in range(2):
                TE(lambda t, half=half: t.matmul(
                    PR[:, half * 512:(half + 1) * 512], ones_col[:],
                    out_scr[1][0:1, half * 512:(half + 1) * 512],
                    start=True, stop=True),
                   r=[ones_col, out_scr[1]], w=[PR], tag=f"bc_b2{half}")
            V(lambda v: v.tensor_copy(b2b[:], PR[:]),
              r=[PR], w=[b2b], tag="bcc_b2")
        hF = make_phase_B(-1, bufB, "bufB")
        specs.append((hF, preF,
                      lambda hook, skip, h=hF:
                      phase_B(-1, bufB, "bufB", out_d, "out", h,
                              is_ffn=True, sandwich=False, tail_hook=hook,
                              skip_head0=skip)))

    def hook_pair(h):
        return (lambda: h[0](0), lambda: h[1](0))

    skip0 = False
    if n_layers > 0:
        emit_prologue(tail_hook=hook_pair(specs[0][0]))
        skip0 = True
    for i, (h, pre, body) in enumerate(specs):
        pre()
        nxt = specs[i + 1][0] if i + 1 < len(specs) else None
        body(hook_pair(nxt) if nxt is not None else None,
             skip0 if i == 0 else True)
    if not use_ffn:
        if dump == "A":
            cur, cur_name = bufA, "bufA"
        for s in range(T // SA):
            t0 = s * SA
            xm = x_tm2[s % 2]
            DMAX(lambda e, t0=t0, src=cur, xm=xm: e.dma_start(
                xm[:], src[t0:t0 + SA, :].rearrange(
                    "(tb p) c -> p tb c", p=128)),
                 r=srckeys(cur_name, t0, 4), w=[xm], tag=f"cp{s}_ld")
            DMAX(lambda e, t0=t0, xm=xm: e.dma_start(
                out_d[t0:t0 + SA, :].rearrange("(tb p) c -> p tb c", p=128),
                xm[:]),
                 r=[xm], w=[("out", s)], tag=f"cp{s}_st")

    sch.finalize()

    with nc.Block() as blk:
        @blk.tensor
        def _(eng):
            sch.emit("tensor", eng)

        @blk.vector
        def _(eng):
            sch.emit("vector", eng)

        @blk.scalar
        def _(eng):
            sch.emit("scalar", eng)

        @blk.gpsimd
        def _(eng):
            sch.emit("gpsimd", eng)

        @blk.sync
        def _(eng):
            sch.emit("sync", eng)
            for sk in (["sync", "gpsimd_dma", "vector_dma", "scalar_dma"]
                       + sorted(sch.relaxed)):
                if sch.totals[sk] > 0:
                    eng.wait_ge(sch.sems[sk], sch.totals[sk])

    return nc


def make_inputs(inputs, n_layers=L):
    def row128(v):
        return np.ascontiguousarray(np.asarray(v).reshape(NCB, 128).T,
                                    dtype=np.float32)

    base = {
        "idf": np.eye(128, dtype=np.float32),
        "ln0w": np.ascontiguousarray(
            np.asarray(inputs["L0_attn_ln0_w"], dtype=np.float32)[None, :]),
        "ln0b": np.ascontiguousarray(
            np.asarray(inputs["L0_attn_ln0_b"], dtype=np.float32)[None, :]),
        "w1": np.ascontiguousarray(inputs["L0_ffn_w1"], dtype=np.float32),
        "b1": np.ascontiguousarray(
            np.asarray(inputs["L0_ffn_b1"],
                       dtype=np.float32).reshape(NFB, 128).T),
        "w2": np.ascontiguousarray(inputs["L0_ffn_w2"], dtype=np.float32),
        "b2": np.ascontiguousarray(
            np.asarray(inputs["L0_ffn_b2"], dtype=np.float32)[None, :]),
    }
    for l in range(L):
        base[f"wkT{l}"] = np.ascontiguousarray(
            np.asarray(inputs["L0_attn_Wk"][l], dtype=np.float32).T)
        base[f"wvT{l}"] = np.ascontiguousarray(
            np.asarray(inputs["L0_attn_Wv"][l], dtype=np.float32).T)
        base[f"wrT{l}"] = np.ascontiguousarray(
            np.asarray(inputs["L0_attn_Wr"][l], dtype=np.float32).T)
        base[f"woT{l}"] = np.ascontiguousarray(
            np.asarray(inputs["L0_attn_Wo"][l], dtype=np.float32).T)
        base[f"wckT{l}"] = np.ascontiguousarray(
            np.asarray(inputs["L0_attn_Wck"][l], dtype=np.float32).T)
        base[f"wcvT{l}"] = np.ascontiguousarray(
            np.asarray(inputs["L0_attn_Wcv"][l], dtype=np.float32).T)
        base[f"wcrT{l}"] = np.ascontiguousarray(
            np.asarray(inputs["L0_attn_Wcr"][l], dtype=np.float32).T)
        base[f"tmk{l}"] = row128(inputs["L0_attn_tmk"][l])
        base[f"tmv{l}"] = row128(inputs["L0_attn_tmv"][l])
        base[f"tmr{l}"] = row128(inputs["L0_attn_tmr"][l])
        base[f"cmk{l}"] = row128(inputs["L0_attn_cmk"][l])
        base[f"cmr{l}"] = row128(inputs["L0_attn_cmr"][l])
        base[f"td{l}"] = row128(inputs["L0_attn_time_decay"][l])
        base[f"tf{l}"] = row128(inputs["L0_attn_time_first"][l])
        base[f"ln1w{l}"] = row128(inputs["L0_attn_ln1_w"][l])
        base[f"ln1b{l}"] = row128(inputs["L0_attn_ln1_b"][l])
        base[f"ln2w{l}"] = row128(inputs["L0_attn_ln2_w"][l])
        base[f"ln2b{l}"] = row128(inputs["L0_attn_ln2_b"][l])
    return base


def kernel(**inputs):
    n = 8
    x = np.ascontiguousarray(inputs["L0_attn_x"], dtype=np.float32)
    base = make_inputs(inputs)
    nc = build_nc()
    in_maps = []
    for b in range(n):
        m = dict(base)
        m["x0"] = np.ascontiguousarray(x[b])
        in_maps.append(m)
    res = run_bass_kernel_spmd(nc, in_maps, core_ids=list(range(n)))
    return np.stack([r["out"] for r in res.results], axis=0)
